# version 11
# baseline (speedup 1.0000x reference)
import numpy as np
import ml_dtypes
import concourse.bass as bass
import concourse.mybir as mybir
from concourse.bass_utils import run_bass_kernel_spmd

F32 = mybir.dt.float32
BF16 = mybir.dt.bfloat16
I32 = mybir.dt.int32
AF = mybir.ActivationFunctionType
ALU = mybir.AluOpType
AX = mybir.AxisListType
bfloat16 = ml_dtypes.bfloat16

L = 2048
D = 1024
TW = 3072
FI = 2816
NT = 16
NG = 8
NKC = 8
NFC = 22

_CONSTS = None

def _build_consts():
    global _CONSTS
    if _CONSTS is not None:
        return _CONSTS
    S1C = np.zeros((128, 264))
    for q in range(4):
        for n1 in range(32):
            p = q * 32 + n1
            for a in range(33):
                ang = 2 * np.pi * n1 * a / 64.0
                S1C[p, q * 66 + a * 2 + 0] = np.cos(ang)
                S1C[p, q * 66 + a * 2 + 1] = -np.sin(ang)
    UH = np.zeros((33, 128, 128))
    for a in range(33):
        for n2 in range(64):
            Tr = np.cos(2 * np.pi * n2 * a / 4096.0)
            Ti = -np.sin(2 * np.pi * n2 * a / 4096.0)
            for g in range(64):
                wr = np.cos(2 * np.pi * n2 * g / 64.0)
                wi = -np.sin(2 * np.pi * n2 * g / 64.0)
                UH[a, 0 * 64 + n2, 0 * 64 + g] = Tr * wr - Ti * wi
                UH[a, 1 * 64 + n2, 0 * 64 + g] = -(Tr * wi + Ti * wr)
                UH[a, 0 * 64 + n2, 1 * 64 + g] = Tr * wi + Ti * wr
                UH[a, 1 * 64 + n2, 1 * 64 + g] = Tr * wr - Ti * wi
    UHs = np.concatenate([UH[a] for a in range(33)], axis=1)
    W1 = np.zeros((128, 128))
    for g in range(64):
        for n2o in range(64):
            c = np.cos(2 * np.pi * n2o * g / 64.0)
            s = np.sin(2 * np.pi * n2o * g / 64.0)
            W1[0 * 64 + g, n2o * 2 + 0] = c
            W1[1 * 64 + g, n2o * 2 + 0] = -s
            W1[0 * 64 + g, n2o * 2 + 1] = s
            W1[1 * 64 + g, n2o * 2 + 1] = c
    W2 = np.zeros((128, 64 * 32))
    for n2o in range(64):
        for a in range(33):
            rho = 1.0 if (a == 0 or a == 32) else 2.0
            for m in range(32):
                t = 64 * m + n2o
                ang = 2 * np.pi * t * a / 4096.0
                W2[a * 2 + 0, n2o * 32 + m] = rho * np.cos(ang) / 4096.0
                W2[a * 2 + 1, n2o * 32 + m] = -rho * np.sin(ang) / 4096.0
    t_hat = np.linspace(0.0, 1.0, L)
    w = 2.0 * np.pi * np.linspace(0.0, L - 1.0, L) / L
    phase = 1e-4 * w
    zT = np.stack([t_hat, np.cos(phase), -np.sin(phase)], axis=0)
    MAX_DECAY = np.log(0.01) / 0.3
    MIN_DECAY = np.log(0.01) / 1.5
    deltas = np.abs(np.linspace(MIN_DECAY, MAX_DECAY, 1024))
    _CONSTS = {
        "cc_s1c": S1C.astype(np.float32),
        "cc_uhat": UHs.astype(np.float32),
        "cc_w1": W1.astype(np.float32),
        "cc_w2": W2.astype(np.float32),
        "cc_zt": np.ascontiguousarray(zT.astype(np.float32)),
        "cc_adelta": np.ascontiguousarray(deltas.reshape(8, 128).T.astype(np.float32)),
        "cc_that": t_hat[None, :].astype(np.float32),
        "cc_iden": np.eye(128, dtype=np.float32),
    }
    return _CONSTS

INPUT_SHAPES = {
    "u": (L, D), "c_row": (1, D),
    "in_w": (D, TW), "swp": (128, 72), "inbp": (128, 24), "sbp": (128, 24),
    "mlp_w0": (3, 64), "mlp_b0": (64, 1), "freq0": (64, 1),
    "mlp_w1": (64, 64), "mlp_b1": (64, 1), "freq1": (64, 1),
    "mlp_w2": (64, 64), "mlp_b2": (64, 1), "freq2": (64, 1),
    "mlp_w3": (64, D), "fbp": (128, 8),
    "out_w": (D, D), "out_b": (1, D),
    "w_gate": (FI, D), "w_up": (FI, D), "w_down": (D, FI),
    "wmod": (D, 6 * D),
    "cc_s1c": (128, 264), "cc_uhat": (128, 33 * 128), "cc_w1": (128, 128),
    "cc_w2": (128, 64 * 32), "cc_zt": (3, L), "cc_adelta": (128, 8),
    "cc_that": (1, L), "cc_iden": (128, 128),
}

BIG = 1 << 60


def build_program(dbg=(), phase_limit=99):
    nc = bass.Bass("TRN2", target_bir_lowering=False)
    IN = {}
    for name, shp in INPUT_SHAPES.items():
        IN[name] = nc.dram_tensor(name, list(shp), F32, kind="ExternalInput").ap()
    out_d = nc.dram_tensor("out", [L, D], F32, kind="ExternalOutput").ap()
    wgqT_d = nc.dram_tensor("wgqT", [D, FI], BF16, kind="Internal").ap()
    wuqT_d = nc.dram_tensor("wuqT", [D, FI], BF16, kind="Internal").ap()
    wdqT_d = nc.dram_tensor("wdqT", [FI, D], BF16, kind="Internal").ap()
    modrow_d = nc.dram_tensor("modrow_d", [1, 6 * D], BF16, kind="Internal").ap()
    w2b16_d = nc.dram_tensor("w2b16_d", [128, 2048], BF16, kind="Internal").ap()

    SEMS = ["cv", "cs", "ct", "cp", "dS0", "dS1", "dS2", "dG0", "dG1", "dG2"]
    ENG_SEM = {"vector": "cv", "scalar": "cs", "tensor": "ct", "gpsimd": "cp"}
    QUEUE_ROT = {"sync": ["dS0", "dS1", "dS2"], "gpsimd": ["dG0", "dG1", "dG2"]}
    QCNT = {"sync": 0, "gpsimd": 0}
    cum = dict.fromkeys(SEMS, 0)
    resW = {}
    resR = {}
    floor = {}
    PEND = {"on": False, "r": [], "w": []}
    PLAN = []

    def _ivl(spec):
        if isinstance(spec, str):
            return (spec, 0, BIG)
        return spec

    def dep(r=(), w=()):
        PEND["r"] = [_ivl(x) for x in r]
        PEND["w"] = [_ivl(x) for x in w]
        PEND["on"] = True

    def st(eng, fn, kind):
        if kind == "dma":
            rot = QUEUE_ROT[eng]
            sem = rot[QCNT[eng] % len(rot)]
            QCNT[eng] += 1
        else:
            sem = ENG_SEM[eng]
        inc = 16 if kind == "dma" else 1
        deps = dict(floor)
        if kind == "dma" and cum[sem] > deps.get(sem, 0):
            deps[sem] = cum[sem]
        if PEND["on"]:
            rs, ws = PEND["r"], PEND["w"]
            PEND["on"] = False
            annotated = True
            for (n, lo, hi) in rs:
                for (l2, h2, s2, v2) in resW.get(n, ()):
                    if l2 < hi and lo < h2:
                        if deps.get(s2, 0) < v2:
                            deps[s2] = v2
            for (n, lo, hi) in ws:
                for lst in (resW, resR):
                    for (l2, h2, s2, v2) in lst.get(n, ()):
                        if l2 < hi and lo < h2:
                            if deps.get(s2, 0) < v2:
                                deps[s2] = v2
        else:
            annotated = False
            for s2 in SEMS:
                if cum[s2] > deps.get(s2, 0):
                    deps[s2] = cum[s2]
        val = cum[sem] + inc
        cum[sem] = val
        if annotated:
            for (n, lo, hi) in rs:
                resR.setdefault(n, []).append((lo, hi, sem, val))
            for (n, lo, hi) in ws:
                for lst in (resW, resR):
                    if n in lst:
                        lst[n] = [e for e in lst[n] if not (lo <= e[0] and e[1] <= hi)]
                resW.setdefault(n, []).append((lo, hi, sem, val))
        else:
            floor.clear()
            floor[sem] = val
            for k in list(resW):
                resW[k] = []
            for k in list(resR):
                resR[k] = []
        PLAN.append((eng, fn, sem, val, deps, inc))

    def D_(eng, out_ap, in_ap, **kw):
        def f(e, o=out_ap, i=in_ap, kw=kw):
            return e.dma_start(out=o, in_=i, **kw)
        st(eng, f, "dma")

    def C_(eng, fn):
        st(eng, fn, "compute")

    def DBG(name, ap, shape, dtype=F32, r=()):
        if name in dbg:
            d = nc.dram_tensor("dbg_" + name, list(shape), dtype, kind="ExternalOutput").ap()
            if r:
                dep(r=r)
            D_("sync", d, ap)

    from contextlib import ExitStack
    _ctx = ExitStack()
    E = _ctx.enter_context
    xres = E(nc.sbuf_tensor("xres", [128, NT, D], BF16))
    hT = E(nc.sbuf_tensor("hT", [128, NKC, L], BF16))
    ycube = E(nc.sbuf_tensor("ycube", [128, NG * L], BF16))
    big1 = E(nc.sbuf_tensor("big1", [128, 12320], BF16))
    big2 = E(nc.sbuf_tensor("big2", [128, 22912], BF16))
    big3 = E(nc.sbuf_tensor("big3", [128, 4096], BF16))
    qa = E(nc.sbuf_tensor("qa", [128, 1024], F32))
    qca = E(nc.sbuf_tensor("qca", [128, 1024], BF16))
    qcb = E(nc.sbuf_tensor("qcb", [128, 1024], BF16))
    qcc = E(nc.sbuf_tensor("qcc", [128, 1024], BF16))
    s1cb = E(nc.sbuf_tensor("s1cb", [128, 264], BF16))
    uhatb = E(nc.sbuf_tensor("uhatb", [128, 33 * 128], BF16))
    w1b = E(nc.sbuf_tensor("w1b", [128, 128], BF16))
    idenb = E(nc.sbuf_tensor("idenb", [128, 128], BF16))
    w3b = E(nc.sbuf_tensor("w3b", [64, D], BF16))
    h3 = E(nc.sbuf_tensor("h3", [64, L], BF16))
    mlpw = E(nc.sbuf_tensor("mlpw", [64, 192], F32))
    mlpc = E(nc.sbuf_tensor("mlpc", [64, 8], F32))
    swt = E(nc.sbuf_tensor("swt", [128, 24, 3], F32))
    inbt = E(nc.sbuf_tensor("inbt", [128, 24], F32))
    bprime = E(nc.sbuf_tensor("bprime", [128, 24], F32))
    fbt = E(nc.sbuf_tensor("fbt", [128, 8], F32))
    acol = E(nc.sbuf_tensor("acol", [128, 8], F32))
    reps = E(nc.sbuf_tensor("reps", [128, 2, D], BF16))
    onesrow = E(nc.sbuf_tensor("onesrow", [1, 128], BF16))
    lns = E(nc.sbuf_tensor("lns", [128, 16], F32))
    qm = E(nc.sbuf_tensor("qm", [128, 64], F32))
    qr = E(nc.sbuf_tensor("qr", [128, 64], F32))
    qsb = E(nc.sbuf_tensor("qsb", [128, 64], BF16))
    ps_a = E(nc.psum_tensor("ps_a", [128, 512], F32))
    ps_b = E(nc.psum_tensor("ps_b", [128, 512], F32))
    ps_w = E(nc.psum_tensor("ps_w", [128, 1024], F32))
    ps_x = E(nc.psum_tensor("ps_x", [128, 1024], F32))
    ps_t1 = E(nc.psum_tensor("ps_t1", [128, 1024], BF16))
    ps_t2 = E(nc.psum_tensor("ps_t2", [128, 1024], BF16))
    SEMOBJ = {s: E(nc.semaphore("sem_" + s)) for s in SEMS}
    block = E(nc.Block())
    if True:
        B1_PAD = [0, 2052, 4104]
        B1_V1 = 6156
        B1_DEC = 8204
        B1_LN = 10252
        B1_H = 11276
        B2_X1 = 0
        B2_VIN = 2048
        B2_X0 = 4096
        B2_SA = 6144
        B2_SB = 10368
        B2_K2 = 14592
        B2_U = B2_SA

        def b2(off, ln):
            return big2[:, off:off + ln]

        def b1(off, ln):
            return big1[:, off:off + ln]

        def r1(off, ln):
            return ("big1", off, off + ln)

        def r2(off, ln):
            return ("big2", off, off + ln)

        def r3(off, ln):
            return ("big3", off, off + ln)

        RYW = lambda lo, hi: ("ycube", lo, hi)

        big1f = big1[:].bitcast(F32)
        ycubef = ycube[:].bitcast(F32)
        big2f = big2[:].bitcast(F32)
        big3f = big3[:].bitcast(F32)
        thatrep = big2f[:, 9408:9408 + 2048]
        R_THAT = r2(18816, 22912)
        modrow = big1[0:1, 0:6144]
        AFC = AF.Copy

        def quant_block(src_ap, dst_ap, W, absf, src_r, dst_r, abs_r, half=None):
            Bn = W // 16
            if half is None:
                qo, bo = 0, 0
            else:
                qo, bo = half * 512, half * 32
            Rq = lambda nm: (nm, bo, bo + Bn)
            Rb = lambda nm: (nm, qo, qo + W)
            qmv = qm[:, bo:bo + Bn]
            qrv = qr[:, bo:bo + Bn]
            qsbv = qsb[:, bo:bo + Bn]
            win = lambda t: t.rearrange("p (nb w) -> p nb w", w=16)
            dep(r=[src_r], w=[Rq("qm")])
            C_("vector", lambda e: e.reduce_max(qmv.unsqueeze(2), win(src_ap), axis=AX.X, apply_absolute_value=True))
            dep(r=[Rq("qm")], w=[Rq("qm")])
            C_("vector", lambda e: e.tensor_scalar_max(qmv, qmv, 6e-12))
            dep(r=[Rq("qm")], w=[Rq("qr")])
            C_("vector", lambda e: e.reciprocal(qrv, qmv))
            dep(r=[Rq("qm")], w=[Rq("qsb")])
            C_("vector", lambda e: e.tensor_scalar(qsbv, qmv, 1.0 / 12.0, None, ALU.mult))
            dep(r=[src_r], w=[abs_r])
            C_("scalar", lambda e: e.activation(absf[:, 0:W], src_ap, AF.Abs))
            abs_res = abs_r
            rbc = qrv.unsqueeze(2).broadcast_to([128, Bn, 16])
            dep(r=[Rq("qr"), abs_res], w=[abs_res])
            C_("vector", lambda e: e.scalar_tensor_tensor(win(absf[:, 0:W]), win(absf[:, 0:W]), 6.0, rbc, ALU.mult, ALU.mult))
            u = absf[:, 0:W]
            A, Bf, Cf = qca[:, qo:qo + W], qcb[:, qo:qo + W], qcc[:, qo:qo + W]
            dep(r=[abs_res], w=[Rb("qca")])
            C_("vector", lambda e: e.tensor_single_scalar(A, u, 0.25, ALU.is_ge))
            for th in (0.75, 1.25, 1.75):
                dep(r=[abs_res, Rb("qca")], w=[Rb("qca")])
                C_("vector", lambda e, th=th: e.scalar_tensor_tensor(A, u, th, A, ALU.is_ge, ALU.add))
            dep(r=[abs_res], w=[Rb("qcb")])
            C_("vector", lambda e: e.tensor_single_scalar(Bf, u, 5.0, ALU.is_ge))
            for th in (5.0, 3.5, 2.5):
                dep(r=[abs_res, Rb("qcb")], w=[Rb("qcb")])
                C_("vector", lambda e, th=th: e.scalar_tensor_tensor(Bf, u, th, Bf, ALU.is_ge, ALU.add))
            dep(r=[Rb("qca"), Rb("qcb")], w=[Rb("qca")])
            C_("vector", lambda e: e.scalar_tensor_tensor(A, Bf, 2.0, A, ALU.mult, ALU.add))
            dep(r=[src_r], w=[Rb("qcb")])
            C_("scalar", lambda e: e.activation(Bf, src_ap, AF.Sign))
            sbc = qsbv.unsqueeze(2).broadcast_to([128, Bn, 16])
            dep(r=[Rb("qcb"), Rq("qsb")], w=[Rb("qcc")])
            C_("gpsimd", lambda e: e.tensor_tensor(win(Cf), win(Bf), sbc, ALU.mult))
            dep(r=[Rb("qca"), Rb("qcc")], w=[dst_r])
            C_("gpsimd", lambda e: e.tensor_mul(dst_ap, A, Cf))

        def load_reps(slots):
            for (j, ri) in slots:
                dep(r=["modrowd"], w=["qcb"])
                D_("sync", qcb[0:1, 0:D], modrow_d[0:1, j * D:(j + 1) * D])
                for hh in range(2):
                    dep(r=["qcb", "onesrow"], w=["ps_w"])
                    C_("tensor", lambda e, hh=hh: e.matmul(ps_w[:, hh * 512:(hh + 1) * 512], onesrow[:], qcb[0:1, hh * 512:(hh + 1) * 512], start=True, stop=True))
                dep(r=["ps_w"], w=[("reps", ri, ri + 1)])
                C_("scalar", lambda e, ri=ri: e.activation(reps[:, ri, :], ps_w[:], AFC))

        dep(w=["xres"])
        D_("gpsimd", xres[:], IN["u"].rearrange("(T p) d -> p T d", p=128))
        dep(w=[r3(0, 256)])
        D_("sync", big3f[:, 0:128], IN["cc_iden"])
        dep(r=[r3(0, 256)], w=["idenb"])
        C_("vector", lambda e: e.tensor_copy(idenb[:], big3f[:, 0:128]))
        for nm, t, rn in (("cc_s1c", s1cb, "s1cb"), ("cc_uhat", uhatb, "uhatb"), ("cc_w1", w1b, "w1b")):
            dep(w=[rn])
            D_("gpsimd", t[:], IN[nm])
        dep(w=[r2(0, 4096)])
        D_("sync", big2f[:, 0:2048], IN["cc_w2"])
        dep(r=[r2(0, 4096)], w=[r2(8192, 10240)])
        C_("vector", lambda e: e.tensor_copy(big2[:, 8192:10240], big2f[:, 0:2048]))
        dep(r=[r2(8192, 10240)], w=["w2b16d"])
        D_("sync", w2b16_d, big2[:, 8192:10240])
        dep(w=["w3b"])
        D_("gpsimd", w3b[:], IN["mlp_w3"])
        dep(w=["swt"])
        D_("sync", swt[:], IN["swp"].rearrange("p (cc k) -> p cc k", k=3))
        dep(w=["inbt"])
        D_("sync", inbt[:], IN["inbp"])
        dep(w=["fbt"])
        D_("sync", fbt[:], IN["fbp"])
        dep(w=["acol"])
        D_("sync", acol[:], IN["cc_adelta"])
        dep(r=["acol"], w=["acol"])
        C_("vector", lambda e: e.tensor_scalar_mul(acol[:], acol[:], -1.0))
        dep(w=["onesrow"])
        C_("vector", lambda e: e.memset(onesrow[:], 1.0))
        dep(w=["bprime"])
        D_("sync", bprime[:], IN["sbp"])
        dep(r=["swt"], w=["qm"])
        C_("vector", lambda e: e.reduce_sum(qm[:, 0:24].unsqueeze(2), swt[:], axis=AX.X))
        dep(r=["inbt", "qm"], w=["qr"])
        C_("vector", lambda e: e.tensor_mul(qr[:, 0:24], inbt[:], qm[:, 0:24]))
        dep(r=["qr", "bprime"], w=["bprime"])
        C_("vector", lambda e: e.tensor_add(bprime[:], bprime[:], qr[:, 0:24]))
        dep(w=[R_THAT])
        D_("sync", thatrep, IN["cc_that"].partition_broadcast(128))

        if phase_limit >= 2:
            scf = big1f[0:1, 3072:4096]
            dep(w=[r1(6144, 8192)])
            D_("sync", scf, IN["c_row"])
            dep(r=[r1(6144, 8192)], w=[r1(8192, 10240)])
            C_("scalar", lambda e: e.activation(big1f[0:1, 4096:5120], scf, AF.Silu))
            dep(r=[r1(8192, 10240)], w=["qcb"])
            C_("vector", lambda e: e.tensor_copy(qcb[0:1, 0:D], big1f[0:1, 4096:5120]))
            for j in range(8):
                dep(r=["qcb", "idenb"], w=["ps_t1"])
                C_("tensor", lambda e, j=j: e.transpose(ps_t1[:, 0:128], qcb[0:1, j * 128:(j + 1) * 128], idenb[0:1, :]))
                dep(r=["ps_t1"], w=["qsb"])
                C_("vector", lambda e, j=j: e.tensor_copy(qsb[:, j:j + 1], ps_t1[:, 0:1]))
            for csk in range(12):
                dep(w=[r3(0, 4096)])
                D_("gpsimd", big3[:, 0:NKC * 512].rearrange("p (kc n) -> p kc n", n=512),
                   IN["wmod"].rearrange("(kc p) n -> p kc n", p=128)[:, :, csk * 512:(csk + 1) * 512])
                for kc in range(NKC):
                    dep(r=["qsb", r3(0, 4096)], w=["ps_a"])
                    C_("tensor", lambda e, kc=kc: e.matmul(
                        ps_a[0:1, 0:512], qsb[:, kc:kc + 1],
                        big3[:, kc * 512:(kc + 1) * 512], start=(kc == 0), stop=(kc == 7)))
                dep(r=["ps_a"], w=[r1(csk * 512, (csk + 1) * 512)])
                C_("scalar", lambda e, csk=csk: e.activation(modrow[:, csk * 512:(csk + 1) * 512], ps_a[0:1, 0:512], AFC))
            for j in (2, 5):
                g_ap = modrow[:, j * D:(j + 1) * D]
                dep(r=[r1(j * D, (j + 1) * D)], w=[r1(8192, 9216)])
                C_("scalar", lambda e, g=g_ap: e.activation(b1(8192, D)[0:1, :], g, AF.Abs))
                dep(r=[r1(8192, 9216)], w=[r1(8192, 9216)])
                C_("vector", lambda e: e.tensor_scalar_max(b1(8192, D)[0:1, :], b1(8192, D)[0:1, :], 0.25))
                dep(r=[r1(j * D, (j + 1) * D)], w=[r1(9216, 10240)])
                C_("scalar", lambda e, g=g_ap: e.activation(b1(9216, D)[0:1, :], g, AF.Sign))
                dep(r=[r1(8192, 10240)], w=[r1(j * D, (j + 1) * D)])
                C_("vector", lambda e, g=g_ap: e.tensor_mul(g, b1(8192, D)[0:1, :], b1(9216, D)[0:1, :]))
            for j in (1, 4):
                g_ap = modrow[:, j * D:(j + 1) * D]
                dep(r=[r1(j * D, (j + 1) * D)], w=[r1(j * D, (j + 1) * D)])
                C_("vector", lambda e, g=g_ap: e.tensor_scalar_add(g, g, 1.0))
            DBG("modrow", modrow, (1, 6 * D), BF16, r=[r1(0, 6144)])
            dep(r=[r1(0, 6144)], w=["modrowd"])
            D_("sync", modrow_d, modrow)

            for jj, (bn, fn_) in enumerate((("mlp_b0", "freq0"), ("mlp_b1", "freq1"), ("mlp_b2", "freq2"))):
                dep(w=[("mlpc", 6, 7)])
                D_("sync", mlpc[:, 6:7], IN[bn])
                dep(w=[("mlpc", 2 * jj + 1, 2 * jj + 2)])
                D_("sync", mlpc[:, 2 * jj + 1:2 * jj + 2], IN[fn_])
                dep(r=[("mlpc", 6, 7), ("mlpc", 2 * jj + 1, 2 * jj + 2)], w=[("mlpc", 2 * jj, 2 * jj + 1)])
                C_("vector", lambda e, jj=jj: e.tensor_mul(mlpc[:, 2 * jj:2 * jj + 1], mlpc[:, 6:7], mlpc[:, 2 * jj + 1:2 * jj + 2]))
            dep(w=[("mlpw", 0, 64)])
            D_("sync", mlpw[:, 0:64], IN["mlp_w1"])
            dep(w=[("mlpw", 64, 128)])
            D_("sync", mlpw[:, 64:128], IN["mlp_w2"])
            dep(w=[("mlpw", 128, 192)])
            D_("sync", mlpw[0:3, 128:192], IN["mlp_w0"])
            ztb = big3f[0:3, 0:L]
            dep(w=[r3(0, 4096)])
            D_("sync", ztb, IN["cc_zt"])
            fsc = big1f[0:64, 0:2048]
            qus = big1f[0:64, 2048:4096]
            sinf = big1f[0:64, 4096:6144]
            qai = qa[:].bitcast(I32)
            RFSC = r1(0, 4096)
            RQUS = r1(4096, 8192)
            RSINF = r1(8192, 12288)

            def sin_layer(ci, outf, outr):
                dep(r=[RFSC, "mlpc"], w=[RFSC])
                C_("vector", lambda e, ci=ci: e.tensor_scalar(fsc, fsc, mlpc[0:64, 2 * ci + 1:2 * ci + 2], mlpc[0:64, 2 * ci:2 * ci + 1], ALU.mult, ALU.add))
                dep(r=[RFSC], w=[RQUS])
                C_("vector", lambda e: e.tensor_scalar_mul(qus, fsc, float(1.0 / (2 * np.pi))))
                for hh in range(2):
                    dep(r=[RQUS], w=["qa"])
                    C_("vector", lambda e, hh=hh: e.tensor_copy(qai[0:64, 0:1024], qus[:, hh * 1024:(hh + 1) * 1024]))
                    dep(r=["qa"], w=[RQUS])
                    C_("vector", lambda e, hh=hh: e.tensor_copy(qus[:, hh * 1024:(hh + 1) * 1024], qai[0:64, 0:1024]))
                dep(r=[RQUS, RFSC], w=[RQUS])
                C_("vector", lambda e: e.scalar_tensor_tensor(qus, qus, float(-2 * np.pi), fsc, ALU.mult, ALU.add))
                dep(r=[RQUS], w=[outr])
                C_("scalar", lambda e, outf=outf: e.activation(outf, qus, AF.Sin))

            for ck in range(4):
                dep(r=["mlpw", r3(0, 4096)], w=["ps_b"])
                C_("tensor", lambda e, ck=ck: e.matmul(ps_b[0:64, 0:512], mlpw[0:3, 128:192], ztb[:, ck * 512:(ck + 1) * 512], start=True, stop=True))
                dep(r=["ps_b"], w=[RFSC])
                C_("scalar", lambda e, ck=ck: e.activation(fsc[:, ck * 512:(ck + 1) * 512], ps_b[0:64, 0:512], AFC))
            DBG("fl1", fsc, (64, L), r=[RFSC])
            sin_layer(0, sinf, RSINF)
            DBG("fs0", sinf, (64, L), r=[RSINF])
            for li in range(2):
                for ck in range(4):
                    dep(r=["mlpw", RSINF], w=["ps_b"])
                    C_("tensor", lambda e, li=li, ck=ck: e.matmul(ps_b[0:64, 0:512], mlpw[0:64, li * 64:(li + 1) * 64], sinf[:, ck * 512:(ck + 1) * 512], start=True, stop=True))
                    dep(r=["ps_b"], w=[RFSC])
                    C_("scalar", lambda e, ck=ck: e.activation(fsc[:, ck * 512:(ck + 1) * 512], ps_b[0:64, 0:512], AFC))
                sin_layer(li + 1, sinf if li == 0 else h3[0:64, :], RSINF if li == 0 else "h3")
            DBG("h3", h3[0:64, :], (64, L), BF16, r=["h3"])

        def ln_stats(xt, tt):
            rx = ("xres", tt, tt + 1)
            dep(r=[rx], w=[r1(B1_LN, B1_LN + D), "lns"])
            C_("scalar", lambda e, xt=xt: e.activation(b1(B1_LN, D), xt, AF.Square, accum_out=lns[:, 0:1]))
            dep(r=[rx, "lns"], w=["lns"])
            C_("vector", lambda e, xt=xt: e.reduce_sum(lns[:, 1:2], xt, axis=AX.X))
            dep(r=["lns"], w=["lns"])
            C_("vector", lambda e: e.tensor_scalar_mul(lns[:, 2:3], lns[:, 1:2], 1.0 / D))
            dep(r=["lns"], w=["lns"])
            C_("vector", lambda e: e.tensor_scalar_mul(lns[:, 3:4], lns[:, 0:1], 1.0 / D))
            dep(r=["lns"], w=["lns"])
            C_("vector", lambda e: e.tensor_mul(lns[:, 4:5], lns[:, 2:3], lns[:, 2:3]))
            dep(r=["lns"], w=["lns"])
            C_("vector", lambda e: e.tensor_sub(lns[:, 5:6], lns[:, 3:4], lns[:, 4:5]))
            dep(r=["lns"], w=["lns"])
            C_("vector", lambda e: e.tensor_scalar_add(lns[:, 5:6], lns[:, 5:6], 1e-5))
            dep(r=["lns"], w=["lns"])
            C_("scalar", lambda e: e.activation(lns[:, 6:7], lns[:, 5:6], AF.Sqrt))
            dep(r=["lns"], w=["lns"])
            C_("vector", lambda e: e.reciprocal(lns[:, 7:8], lns[:, 6:7]))
            dep(r=["lns"], w=["lns"])
            C_("vector", lambda e: e.tensor_mul(lns[:, 8:9], lns[:, 2:3], lns[:, 7:8]))
            dep(r=["lns"], w=["lns"])
            C_("vector", lambda e: e.tensor_scalar_mul(lns[:, 8:9], lns[:, 8:9], -1.0))

        if phase_limit >= 3:
            load_reps([(1, 0), (0, 1)])
            for tt in range(NT):
                xt = xres[:, tt, :]
                rx = ("xres", tt, tt + 1)
                rht = ("hT", tt * 128, (tt + 1) * 128)
                ln_stats(xt, tt)
                dep(r=[rx, "lns"], w=[r1(B1_H, B1_H + D)])
                C_("scalar", lambda e, xt=xt: e.activation(b1(B1_H, D), xt, AF.Identity, bias=lns[:, 8:9], scale=lns[:, 7:8]))
                dep(r=[r1(B1_H, B1_H + D), ("reps", 0, 1)], w=[r1(B1_H, B1_H + D)])
                C_("vector", lambda e: e.tensor_mul(b1(B1_H, D), b1(B1_H, D), reps[:, 0, :]))
                dep(r=[r1(B1_H, B1_H + D), ("reps", 1, 2)], w=[r1(B1_H, B1_H + D)])
                C_("vector", lambda e: e.tensor_add(b1(B1_H, D), b1(B1_H, D), reps[:, 1, :]))
                for kc in range(NKC):
                    dep(r=[r1(B1_H, B1_H + D), "idenb"], w=["ps_t2"])
                    C_("tensor", lambda e, kc=kc: e.transpose(ps_t2[:, kc * 128:(kc + 1) * 128], b1(B1_H + kc * 128, 128), idenb[:]))
                dep(r=["ps_t2"], w=[rht])
                C_("scalar", lambda e, tt=tt: e.activation(hT[:, :, tt * 128:(tt + 1) * 128], ps_t2[:].rearrange("p (kc n) -> p kc n", n=128), AFC))
            DBG("hT", hT[:].rearrange("p kc t -> p (kc t)"), (128, NKC * L), BF16, r=["hT"])

        WQ0 = 10240
        wq_emitters = []

        def _emit_gu_slice(wname, dst, dstr, sl):
            def em():
                wsrc = ycubef[:, 4096:4096 + D]
                dep(w=[RYW(8192, 10240)])
                D_("sync", wsrc, IN[wname][sl * 128:(sl + 1) * 128, :])
                quant_block(wsrc, ycube[:, WQ0:WQ0 + D], D, qa, RYW(8192, 10240), RYW(WQ0, WQ0 + D), "qa")
                for kc in range(NKC):
                    dep(r=[RYW(WQ0 + kc * 128, WQ0 + (kc + 1) * 128), "idenb"], w=["ps_t2"])
                    C_("tensor", lambda e, kc=kc: e.transpose(ps_t2[:, kc * 128:(kc + 1) * 128], ycube[:, WQ0 + kc * 128:WQ0 + (kc + 1) * 128], idenb[:]))
                dep(r=["ps_t2"], w=[RYW(13056, 14080)])
                C_("vector", lambda e: e.tensor_copy(ycube[:, 13056:14080], ps_t2[:]))
                dep(r=[RYW(13056, 14080)], w=[(dstr, sl, sl + 1)])
                D_("sync", dst.rearrange("(kc p) n -> p kc n", p=128)[:, :, sl * 128:(sl + 1) * 128],
                   ycube[:, 13056:14080].rearrange("p (kc n) -> p kc n", n=128))
            return em

        def _emit_wd_slice(sl):
            def em():
                for pc, pw in ((0, 1024), (1024, 1024), (2048, 768)):
                    dep(w=[RYW(8192, 10240)])
                    D_("sync", ycubef[:, 4096:4096 + pw], IN["w_down"][sl * 128:(sl + 1) * 128, pc:pc + pw])
                    quant_block(ycubef[:, 4096:4096 + pw], ycube[:, WQ0 + pc:WQ0 + pc + pw], pw, qa, RYW(8192, 10240), RYW(WQ0 + pc, WQ0 + pc + pw), "qa")
                for fc in range(NFC):
                    dep(r=[RYW(WQ0 + fc * 128, WQ0 + (fc + 1) * 128), "idenb"], w=["ps_t2"])
                    C_("tensor", lambda e, fc=fc: e.transpose(ps_t2[:, (fc % 8) * 128:(fc % 8 + 1) * 128], ycube[:, WQ0 + fc * 128:WQ0 + (fc + 1) * 128], idenb[:]))
                    if fc % 8 == 7 or fc == NFC - 1:
                        n0 = (fc // 8) * 8
                        cnt = fc - n0 + 1
                        dep(r=["ps_t2"], w=[RYW(13056 + n0 * 128, 13056 + (n0 + cnt) * 128)])
                        C_("vector", lambda e, n0=n0, cnt=cnt: e.tensor_copy(
                            ycube[:, 13056 + n0 * 128:13056 + (n0 + cnt) * 128], ps_t2[:, 0:cnt * 128]))
                dep(r=[RYW(13056, 13056 + NFC * 128)], w=[("wdqT", sl, sl + 1)])
                D_("sync", wdqT_d.rearrange("(fc p) n -> p fc n", p=128)[:, :, sl * 128:(sl + 1) * 128],
                   ycube[:, 13056:13056 + NFC * 128].rearrange("p (fc n) -> p fc n", n=128))
            return em

        if phase_limit >= 1:
            for (wname, dst, dstr) in (("w_gate", wgqT_d, "wgqT"), ("w_up", wuqT_d, "wuqT")):
                for sl in range(NFC):
                    wq_emitters.append(_emit_gu_slice(wname, dst, dstr, sl))
            for sl in range(NKC):
                wq_emitters.append(_emit_wd_slice(sl))

        def drain_wq(k):
            for _ in range(k):
                if wq_emitters:
                    wq_emitters.pop(0)()

        def fwd_fft(src_off, dst_off):
            src = b2(src_off, L)
            rsrc = r2(src_off, src_off + L)
            for Q in range(16):
                pst, pstr = (ps_t1, "ps_t1") if (Q // 8) % 2 == 0 else (ps_t2, "ps_t2")
                for q in range(4):
                    n2 = 4 * Q + q
                    inap = src.rearrange("p (n1 n2) -> p n2 n1", n2=64)[:, n2, :]
                    dep(r=[rsrc, "idenb"], w=[pstr])
                    C_("tensor", lambda e, q=q, Q=Q, inap=inap, pst=pst: e.transpose(
                        pst[32 * q:32 * (q + 1), (Q % 8) * 128:(Q % 8 + 1) * 128], inap, idenb[:],
                        tile_position=(0, 32 * q)))
                if Q % 8 == 7:
                    dep(r=[pstr], w=[r1(B1_V1 + (Q - 7) * 128, B1_V1 + (Q + 1) * 128)])
                    C_("vector", lambda e, Q=Q, pst=pst: e.tensor_copy(
                        big1[:, B1_V1 + (Q - 7) * 128:B1_V1 + (Q + 1) * 128], pst[:]))
            for Q in range(16):
                ps, psr = (ps_a, "ps_a") if Q % 2 == 0 else (ps_b, "ps_b")
                dep(r=[r1(B1_V1 + Q * 128, B1_V1 + (Q + 1) * 128), "s1cb"], w=[psr])
                C_("tensor", lambda e, Q=Q, ps=ps: e.matmul(ps[:, 0:264], big1[:, B1_V1 + Q * 128:B1_V1 + (Q + 1) * 128], s1cb[:], start=True, stop=True))
                Aap = b2(B2_SA, 4224).rearrange("p (a r n2) -> p a r n2", a=33, r=2)[:, :, :, 4 * Q:4 * Q + 4].transpose([0, 3, 1, 2])
                dep(r=[psr], w=[r2(B2_SA, B2_SA + 4224)])
                C_("scalar", lambda e, Aap=Aap, ps=ps: e.activation(Aap, ps[:, 0:264].rearrange("p (q a r) -> p q a r", q=4, a=33), AFC))
            for a in range(33):
                pst, pstr = (ps_t1, "ps_t1") if (a // 8) % 2 == 0 else (ps_t2, "ps_t2")
                inap = b2(B2_SA + a * 128, 128)
                dep(r=[r2(B2_SA + a * 128, B2_SA + (a + 1) * 128), "idenb"], w=[pstr])
                C_("tensor", lambda e, a=a, inap=inap, pst=pst: e.transpose(pst[:, (a % 8) * 128:(a % 8 + 1) * 128], inap, idenb[:]))
                if a % 8 == 7 or a == 32:
                    a0 = (a // 8) * 8
                    cnt = a - a0 + 1
                    dep(r=[pstr], w=[r2(B2_SB + a0 * 128, B2_SB + (a0 + cnt) * 128)])
                    C_("vector", lambda e, a0=a0, cnt=cnt, pst=pst: e.tensor_copy(
                        big2[:, B2_SB + a0 * 128:B2_SB + (a0 + cnt) * 128], pst[:, 0:cnt * 128]))
            for a in range(33):
                ps, psr = (ps_a, "ps_a") if (a // 4) % 2 == 0 else (ps_b, "ps_b")
                dep(r=[r2(B2_SB + a * 128, B2_SB + (a + 1) * 128), "uhatb"], w=[psr])
                C_("tensor", lambda e, a=a, ps=ps: e.matmul(ps[:, (a % 4) * 128:(a % 4 + 1) * 128], big2[:, B2_SB + a * 128:B2_SB + (a + 1) * 128], uhatb[:, a * 128:(a + 1) * 128], start=True, stop=True))
                if a % 4 == 3 or a == 32:
                    a0 = (a // 4) * 4
                    cnt = a - a0 + 1
                    dep(r=[psr], w=[r2(dst_off + a0 * 128, dst_off + (a0 + cnt) * 128)])
                    C_("scalar", lambda e, a0=a0, cnt=cnt, ps=ps: e.activation(b2(dst_off + a0 * 128, cnt * 128), ps[:, 0:cnt * 128], AFC))

        def inv_fft(y_off, g):
            for a in range(33):
                pst, pstr = (ps_t1, "ps_t1") if (a // 8) % 2 == 0 else (ps_t2, "ps_t2")
                inap = b2(y_off + a * 128, 128)
                dep(r=[r2(y_off + a * 128, y_off + (a + 1) * 128), "idenb"], w=[pstr])
                C_("tensor", lambda e, a=a, inap=inap, pst=pst: e.transpose(pst[:, (a % 8) * 128:(a % 8 + 1) * 128], inap, idenb[:]))
                if a % 8 == 7 or a == 32:
                    a0 = (a // 8) * 8
                    cnt = a - a0 + 1
                    dep(r=[pstr], w=[r2(B2_SB + a0 * 128, B2_SB + (a0 + cnt) * 128)])
                    C_("vector", lambda e, a0=a0, cnt=cnt, pst=pst: e.tensor_copy(
                        big2[:, B2_SB + a0 * 128:B2_SB + (a0 + cnt) * 128], pst[:, 0:cnt * 128]))
            for a in range(33):
                ps, psr = (ps_a, "ps_a") if (a // 4) % 2 == 0 else (ps_b, "ps_b")
                dep(r=[r2(B2_SB + a * 128, B2_SB + (a + 1) * 128), "w1b"], w=[psr])
                C_("tensor", lambda e, a=a, ps=ps: e.matmul(ps[:, (a % 4) * 128:(a % 4 + 1) * 128], big2[:, B2_SB + a * 128:B2_SB + (a + 1) * 128], w1b[:], start=True, stop=True))
                if a % 4 == 3 or a == 32:
                    a0 = (a // 4) * 4
                    cnt = a - a0 + 1
                    Dap = b2(B2_K2, 4224).rearrange("p (n2o a r) -> p n2o a r", n2o=64, a=33)[:, :, a0:a0 + cnt, :].transpose([0, 2, 1, 3])
                    dep(r=[psr], w=[r2(B2_K2, B2_K2 + 4224)])
                    C_("scalar", lambda e, Dap=Dap, cnt=cnt, ps=ps: e.activation(Dap, ps[:, 0:cnt * 128].rearrange("p (a4 n2o r) -> p a4 n2o r", n2o=64, r=2), AFC))
            for strip in range(8):
                for j in range(8):
                    n2o = strip * 8 + j
                    inap = b2(B2_K2 + n2o * 66, 66)
                    dep(r=[r2(B2_K2 + n2o * 66, B2_K2 + n2o * 66 + 66), "idenb"], w=["ps_t2"])
                    C_("tensor", lambda e, j=j, inap=inap: e.transpose(ps_t2[0:66, j * 128:(j + 1) * 128], inap, idenb[:]))
                dep(r=["ps_t2"], w=[r1(B1_V1, B1_V1 + 1024)])
                C_("vector", lambda e: e.tensor_copy(big1[0:66, B1_V1:B1_V1 + 1024], ps_t2[0:66, :]))
                for j in range(8):
                    n2o = strip * 8 + j
                    dep(r=[r1(B1_V1, B1_V1 + 1024), r1(B1_DEC, B1_DEC + 2048)], w=["ps_w"])
                    C_("tensor", lambda e, j=j, n2o=n2o: e.matmul(ps_w[:, j * 32:(j + 1) * 32], big1[0:66, B1_V1 + j * 128:B1_V1 + (j + 1) * 128], big1[0:66, B1_DEC + n2o * 32:B1_DEC + (n2o + 1) * 32], start=True, stop=True))
                def tv(t, s):
                    return t.rearrange("p (m n2) -> p n2 m", n2=64)[:, s * 8:(s + 1) * 8, :]
                dep(r=["ps_w", r2(B2_VIN, B2_VIN + L), "fbt"], w=[("ycube", g * L, (g + 1) * L)])
                C_("vector", lambda e, s=strip: e.scalar_tensor_tensor(
                    tv(ycube[:, g * L:(g + 1) * L], s), tv(b2(B2_VIN, L), s), fbt[:, g:g + 1],
                    ps_w[:, 0:256].rearrange("p (j m) -> p j m", j=8), ALU.mult, ALU.add))
            dep(r=[("ycube", g * L, (g + 1) * L), r2(B2_X0, B2_X0 + L)], w=[("ycube", g * L, (g + 1) * L)])
            C_("vector", lambda e: e.tensor_mul(ycube[:, g * L:(g + 1) * L], ycube[:, g * L:(g + 1) * L], b2(B2_X0, L)))

        if phase_limit >= 4:
            inw3 = big3[:, 0:NKC * 384].rearrange("p (kc s n) -> p kc s n", s=3, n=128)

            def inproj(g):
                for s_ in range(3):
                    dep(w=[r3(s_ * 1024, s_ * 1024 + 1024)])
                    D_("gpsimd", inw3[:, :, s_, :],
                       IN["in_w"].rearrange("(kc p) (s nn) -> p kc s nn", p=128, nn=1024)[:, :, s_, g * 128:(g + 1) * 128])
                for s in range(3):
                    cc = s * 8 + g
                    dep(r=["inbt"], w=["lns"])
                    C_("vector", lambda e, cc=cc: e.tensor_scalar_mul(lns[:, 10:11], inbt[:, cc:cc + 1], -1.0))
                    dep(r=["lns"], w=[r1(B1_PAD[s], B1_PAD[s] + 2)])
                    C_("vector", lambda e, s=s: e.tensor_copy(big1[:, B1_PAD[s]:B1_PAD[s] + 2], lns[:, 10:11].broadcast_to([128, 2])))
                    for tc in range(4):
                        pso = ps_x[:, 0:512] if tc % 2 == 0 else ps_x[:, 512:1024]
                        psr = "ps_x"
                        for kc in range(NKC):
                            dep(r=[r3(0, 3072), ("hT", tc * 512, (tc + 1) * 512)], w=[psr])
                            C_("tensor", lambda e, s=s, tc=tc, kc=kc, g=g, pso=pso: e.matmul(
                                pso, inw3[:, kc, s, :],
                                hT[:, kc, tc * 512:(tc + 1) * 512], start=(kc == 0), stop=(kc == 7)))
                        dep(r=[psr], w=[r1(B1_PAD[s] + 2 + tc * 512, B1_PAD[s] + 2 + (tc + 1) * 512)])
                        C_("scalar", lambda e, s=s, tc=tc, pso=pso: e.activation(
                            big1[:, B1_PAD[s] + 2 + tc * 512:B1_PAD[s] + 2 + (tc + 1) * 512], pso, AFC))

            inproj(0)
            for g in range(NG):
                for s in range(3):
                    cc = s * 8 + g
                    dsts = [B2_X0, B2_X1, B2_VIN][s]
                    rpad = r1(B1_PAD[s], B1_PAD[s] + 2 + L)
                    rdst = r2(dsts, dsts + L)
                    dep(r=[rpad, "swt", "bprime"], w=[rdst])
                    C_("vector", lambda e, s=s, cc=cc, dsts=dsts: e.tensor_scalar(
                        b2(dsts, L), big1[:, B1_PAD[s]:B1_PAD[s] + L],
                        swt[:, cc, 0:1], bprime[:, cc:cc + 1], ALU.mult, ALU.add))
                    dep(r=[rpad, "swt", rdst], w=[rdst])
                    C_("vector", lambda e, s=s, cc=cc, dsts=dsts: e.scalar_tensor_tensor(
                        b2(dsts, L), big1[:, B1_PAD[s] + 1:B1_PAD[s] + 1 + L],
                        swt[:, cc, 1:2], b2(dsts, L), ALU.mult, ALU.add))
                    dep(r=[rpad, "swt", rdst], w=[rdst])
                    C_("vector", lambda e, s=s, cc=cc, dsts=dsts: e.scalar_tensor_tensor(
                        b2(dsts, L), big1[:, B1_PAD[s] + 2:B1_PAD[s] + 2 + L],
                        swt[:, cc, 2:3], b2(dsts, L), ALU.mult, ALU.add))
                dep(r=[r2(B2_VIN, B2_VIN + L), r2(B2_X1, B2_X1 + L)], w=[r2(B2_VIN, B2_VIN + L)])
                C_("vector", lambda e: e.tensor_mul(b2(B2_VIN, L), b2(B2_VIN, L), b2(B2_X1, L)))
                if g == 0:
                    DBG("vin0", b2(B2_VIN, L), (128, L), BF16, r=[r2(B2_VIN, B2_VIN + L)])
                    DBG("x00", b2(B2_X0, L), (128, L), BF16, r=[r2(B2_X0, B2_X0 + L)])
                dep(r=[R_THAT, "acol"], w=[r1(B1_DEC, B1_DEC + L)])
                C_("scalar", lambda e, g=g: e.activation(b1(B1_DEC, L), thatrep, AF.Exp, scale=acol[:, g:g + 1]))
                for qh in range(4):
                    pso, psr = (ps_a, "ps_a") if qh % 2 == 0 else (ps_b, "ps_b")
                    dep(r=["w3b", "h3"], w=[psr])
                    C_("tensor", lambda e, g=g, qh=qh, pso=pso: e.matmul(
                        pso[:, 0:512], w3b[0:64, g * 128:(g + 1) * 128], h3[0:64, qh * 512:(qh + 1) * 512], start=True, stop=True))
                    dep(r=[psr, r1(B1_DEC + qh * 512, B1_DEC + (qh + 1) * 512)], w=[r2(B2_X1 + qh * 512, B2_X1 + (qh + 1) * 512)])
                    C_("vector", lambda e, qh=qh, pso=pso: e.tensor_mul(big2[:, B2_X1 + qh * 512:B2_X1 + (qh + 1) * 512], pso[:, 0:512], big1[:, B1_DEC + qh * 512:B1_DEC + (qh + 1) * 512]))
                if g == 0:
                    DBG("kdense0", b2(B2_X1, L), (128, L), BF16, r=[r2(B2_X1, B2_X1 + L)])
                fwd_fft(B2_X1, B2_K2)
                fwd_fft(B2_VIN, B2_U)
                if g == 0:
                    DBG("U0", b2(B2_U, 4224), (128, 4224), BF16, r=[r2(B2_U, B2_U + 4224)])
                    DBG("K0", b2(B2_K2, 4224), (128, 4224), BF16, r=[r2(B2_K2, B2_K2 + 4224)])
                def vw(off, r):
                    return b2(off, 4224).rearrange("p (a r n) -> p a r n", a=33, r=2)[:, :, r, :]
                RU = r2(B2_U, B2_U + 4224)
                RK = r2(B2_K2, B2_K2 + 4224)
                sp1 = b2(B2_SB, 2112).rearrange("p (a n) -> p a n", a=33)
                sp2 = b2(B2_SB + 2112, 2112).rearrange("p (a n) -> p a n", a=33)
                dep(r=[RU, RK], w=[r2(B2_SB, B2_SB + 2112)])
                C_("vector", lambda e: e.tensor_tensor(sp1, vw(B2_U, 0), vw(B2_K2, 1), ALU.mult))
                dep(r=[RU, RK], w=[r2(B2_SB + 2112, B2_SB + 4224)])
                C_("gpsimd", lambda e: e.tensor_tensor(sp2, vw(B2_U, 1), vw(B2_K2, 1), ALU.mult))
                dep(r=[RU, RK], w=[RU])
                C_("vector", lambda e: e.tensor_tensor(vw(B2_U, 1), vw(B2_U, 1), vw(B2_K2, 0), ALU.mult))
                dep(r=[RU, RK], w=[RU])
                C_("gpsimd", lambda e: e.tensor_tensor(vw(B2_U, 0), vw(B2_U, 0), vw(B2_K2, 0), ALU.mult))
                dep(r=[RU, r2(B2_SB + 2112, B2_SB + 4224)], w=[RU])
                C_("gpsimd", lambda e: e.tensor_tensor(vw(B2_U, 0), vw(B2_U, 0), sp2, ALU.subtract))
                dep(r=[RU, r2(B2_SB, B2_SB + 2112)], w=[RU])
                C_("vector", lambda e: e.tensor_tensor(vw(B2_U, 1), vw(B2_U, 1), sp1, ALU.add))
                if g == 0:
                    DBG("Ymul", b2(B2_U, 4224), (128, 4224), BF16, r=[RU])
                if g + 1 < NG:
                    inproj(g + 1)
                dep(r=["w2b16d"], w=[r1(B1_DEC, B1_DEC + 2048)])
                D_("sync", big1[:, B1_DEC:B1_DEC + 2048], w2b16_d)
                if g == 0:
                    DBG("w2g", big1[:, B1_DEC:B1_DEC + 2048], (128, 2048), BF16, r=[r1(B1_DEC, B1_DEC + 2048)])
                inv_fft(B2_U, g)
                if g == 0:
                    DBG("y0", ycube[:, 0:L], (128, L), BF16, r=[("ycube", 0, L)])
                drain_wq(13)

        if phase_limit >= 5:
            load_reps([(2, 0)])
            for hf in range(2):
                dep(w=[r3(0, 4096)])
                D_("gpsimd", big3[:, 0:4096].rearrange("p (kc n) -> p kc n", n=1024),
                   IN["out_w"].rearrange("(kc p) n -> p kc n", p=128)[:, hf * 4:(hf + 1) * 4, :])
                for kc4 in range(4):
                    kc = hf * 4 + kc4
                    dep(r=[r3(kc4 * 1024, (kc4 + 1) * 1024), ("reps", 0, 1)], w=[r1(kc * 1024, (kc + 1) * 1024)])
                    C_("vector", lambda e, kc=kc, kc4=kc4: e.tensor_mul(big1[:, kc * 1024:(kc + 1) * 1024], big3[:, kc4 * 1024:(kc4 + 1) * 1024], reps[:, 0, :]))
            dep(w=["qa"])
            D_("sync", qa[0:1, 0:1024], IN["out_b"])
            dep(r=["qa"], w=["qcb"])
            C_("vector", lambda e: e.tensor_copy(qcb[0:1, 0:1024], qa[0:1, 0:1024]))
            for hh in range(2):
                dep(r=["qcb", "onesrow"], w=["ps_w"])
                C_("tensor", lambda e, hh=hh: e.matmul(ps_w[:, hh * 512:(hh + 1) * 512], onesrow[:], qcb[0:1, hh * 512:(hh + 1) * 512], start=True, stop=True))
            dep(r=["ps_w", ("reps", 0, 1)], w=[r1(8192, 9216)])
            C_("vector", lambda e: e.tensor_mul(b1(8192, 1024), ps_w[:].rearrange("p n -> p n"), reps[:, 0, :]))
            for tt in range(NT):
                psp, pspr = (ps_w, "ps_w") if tt % 2 == 0 else (ps_x, "ps_x")
                for kc in range(NKC):
                    for hh in range(2):
                        dep(r=["ycube", r1(0, 8192)], w=[pspr])
                        C_("tensor", lambda e, tt=tt, kc=kc, hh=hh, psp=psp: e.matmul(
                            psp[:, hh * 512:(hh + 1) * 512], ycube[:, kc * L + tt * 128:kc * L + (tt + 1) * 128],
                            big1[:, kc * 1024 + hh * 512:kc * 1024 + (hh + 1) * 512],
                            start=(kc == 0), stop=(kc == 7)))
                dep(r=[pspr, ("xres", tt, tt + 1)], w=[("xres", tt, tt + 1)])
                C_("vector", lambda e, tt=tt, psp=psp: e.tensor_add(xres[:, tt, :], xres[:, tt, :], psp[:]))
                dep(r=[r1(8192, 9216), ("xres", tt, tt + 1)], w=[("xres", tt, tt + 1)])
                C_("vector", lambda e, tt=tt: e.tensor_add(xres[:, tt, :], xres[:, tt, :], b1(8192, 1024)))
            DBG("xattn", xres[:].rearrange("p T d -> p (T d)"), (128, NT * D), BF16, r=["xres"])

        drain_wq(99)
        if phase_limit >= 6:
            load_reps([(4, 0), (3, 1)])
            for tt in range(NT):
                xt = xres[:, tt, :]
                rx = ("xres", tt, tt + 1)
                rht = ("hT", tt * 128, (tt + 1) * 128)
                ln_stats(xt, tt)
                dep(r=[rx, "lns"], w=["qa"])
                C_("scalar", lambda e, xt=xt: e.activation(qa[:, 0:D], xt, AF.Identity, bias=lns[:, 8:9], scale=lns[:, 7:8]))
                dep(r=["qa", ("reps", 0, 1)], w=["qa"])
                C_("vector", lambda e: e.tensor_mul(qa[:, 0:D], qa[:, 0:D], reps[:, 0, :]))
                dep(r=["qa", ("reps", 1, 2)], w=["qa"])
                C_("vector", lambda e: e.tensor_add(qa[:, 0:D], qa[:, 0:D], reps[:, 1, :]))
                quant_block(qa[:, 0:512], qcb[:, 0:512], 512, big1f[:, 0:512], ("qa", 0, 512), ("qcb", 0, 512), r1(0, 1024), half=0)
                quant_block(qa[:, 512:1024], qcb[:, 512:1024], 512, big1f[:, 512:1024], ("qa", 512, 1024), ("qcb", 512, 1024), r1(1024, 2048), half=1)
                for kc in range(NKC):
                    dep(r=["qcb", "idenb"], w=["ps_t2"])
                    C_("tensor", lambda e, kc=kc: e.transpose(ps_t2[:, kc * 128:(kc + 1) * 128], qcb[:, kc * 128:(kc + 1) * 128], idenb[:]))
                dep(r=["ps_t2"], w=[rht])
                C_("scalar", lambda e, tt=tt: e.activation(hT[:, :, tt * 128:(tt + 1) * 128], ps_t2[:].rearrange("p (kc n) -> p kc n", n=128), AFC))
            DBG("h2qT", hT[:].rearrange("p kc t -> p (kc t)"), (128, NKC * L), BF16, r=["hT"])

        if phase_limit >= 7:
            def hqT_ap(tth, fc):
                idx = tth * NFC + fc
                if idx < 128:
                    return ycube[:, idx * 128:(idx + 1) * 128], ("ycube", idx * 128, (idx + 1) * 128)
                idx -= 128
                return big1[:, idx * 128:(idx + 1) * 128], ("big1", idx * 128, (idx + 1) * 128)

            load_reps([(5, 0)])
            dep(r=["wdqT"], w=["big2"])
            D_("sync", big2[:, 0:NFC * 1024].rearrange("p (fc n) -> p fc n", n=1024),
               wdqT_d.rearrange("(fc p) n -> p fc n", p=128))
            wgs = big3[:, 0:4096]
            wus = big1[:, 8192:12288]

            def dproj(half, tth):
                tt = half * 8 + tth
                ho2 = (tth % 2) * 512
                psp, pspr = (ps_w, "ps_w") if tth % 2 == 0 else (ps_x, "ps_x")
                for fc in range(NFC):
                    lh, lhr = hqT_ap(tth, fc)
                    for hh in range(2):
                        dep(r=[lhr, ("big2", fc * 1024, (fc + 1) * 1024)], w=[pspr])
                        C_("tensor", lambda e, fc=fc, hh=hh, lh=lh, psp=psp: e.matmul(
                            psp[:, hh * 512:(hh + 1) * 512], lh,
                            big2[:, fc * 1024 + hh * 512:fc * 1024 + (hh + 1) * 512],
                            start=(fc == 0), stop=(fc == NFC - 1)))
                for hh in range(2):
                    dep(r=[pspr, ("reps", 0, 1)], w=[("qa", ho2, ho2 + 512)])
                    C_("vector", lambda e, hh=hh, ho2=ho2, psp=psp: e.tensor_mul(
                        qa[:, ho2:ho2 + 512], psp[:, hh * 512:(hh + 1) * 512], reps[:, 0, hh * 512:(hh + 1) * 512]))
                    dep(r=[("qa", ho2, ho2 + 512), ("xres", tt, tt + 1)], w=[("qa", ho2, ho2 + 512)])
                    C_("vector", lambda e, hh=hh, ho2=ho2, tt=tt: e.tensor_add(
                        qa[:, ho2:ho2 + 512], qa[:, ho2:ho2 + 512], xres[:, tt, hh * 512:(hh + 1) * 512]))
                    dep(r=[("qa", ho2, ho2 + 512)], w=[("outd", tt, tt + 1)])
                    D_("sync", out_d.rearrange("(T p) d -> p T d", p=128)[:, tt, hh * 512:(hh + 1) * 512], qa[:, ho2:ho2 + 512])
            CHUNKS = [(i * 512, 512) for i in range(5)] + [(2560, 256)]
            chain_i = 0
            for half in range(2):
                for (n0, cw) in CHUNKS:
                    dep(r=["wgqT"], w=[r3(0, 4096)])
                    D_("sync", wgs[:, 0:NKC * cw].rearrange("p (kc n) -> p kc n", n=cw),
                       wgqT_d.rearrange("(kc p) n -> p kc n", p=128)[:, :, n0:n0 + cw])
                    dep(r=["wuqT"], w=[r1(8192, 12288)])
                    D_("gpsimd", wus[:, 0:NKC * cw].rearrange("p (kc n) -> p kc n", n=cw),
                       wuqT_d.rearrange("(kc p) n -> p kc n", p=128)[:, :, n0:n0 + cw])
                    for tth in range(8):
                        tt = half * 8 + tth
                        hq = chain_i % 2
                        chain_i += 1
                        ho = hq * 512
                        psg, psgr = (ps_a, "ps_a") if hq == 0 else (ps_b, "ps_b")
                        psu, psur = (ps_w, "ps_w") if hq == 0 else (ps_x, "ps_x")
                        pst, pstr = (ps_t2, "ps_t2") if hq == 0 else (ps_t1, "ps_t1")
                        rht = ("hT", tt * 128, (tt + 1) * 128)
                        for kc in range(NKC):
                            dep(r=[rht, r3(0, 4096)], w=[psgr])
                            C_("tensor", lambda e, tt=tt, kc=kc, cw=cw, psg=psg: e.matmul(
                                psg[:, 0:cw], hT[:, kc, tt * 128:(tt + 1) * 128],
                                wgs[:, kc * cw:(kc + 1) * cw], start=(kc == 0), stop=(kc == 7)))
                        for kc in range(NKC):
                            dep(r=[rht, r1(8192, 12288)], w=[psur])
                            C_("tensor", lambda e, tt=tt, kc=kc, cw=cw, psu=psu: e.matmul(
                                psu[:, 0:cw], hT[:, kc, tt * 128:(tt + 1) * 128],
                                wus[:, kc * cw:(kc + 1) * cw], start=(kc == 0), stop=(kc == 7)))
                        dep(r=[psgr], w=[("qcc", ho, ho + cw)])
                        C_("scalar", lambda e, psg=psg, ho=ho, cw=cw: e.activation(qcc[:, ho:ho + cw], psg[:, 0:cw], AF.Silu))
                        dep(r=[("qcc", ho, ho + cw), psur], w=[("qa", ho, ho + cw)])
                        C_("vector", lambda e, psu=psu, ho=ho, cw=cw: e.tensor_tensor(qa[:, ho:ho + cw], qcc[:, ho:ho + cw], psu[:, 0:cw], ALU.mult))
                        if half == 0 and n0 == 0 and tth == 0:
                            DBG("hmid0", qa[:, 0:256], (128, 256), r=["qa"])
                        quant_block(qa[:, ho:ho + cw], qcb[:, ho:ho + cw], cw, big1f[:, 3072 + hq * 512:3584 + hq * 512], ("qa", ho, ho + cw), ("qcb", ho, ho + cw), r1(6144 + hq * 1024, 7168 + hq * 1024), half=hq)
                        for j in range(cw // 128):
                            dep(r=[("qcb", ho, ho + cw), "idenb"], w=[pstr])
                            C_("tensor", lambda e, j=j, ho=ho, pst=pst: e.transpose(pst[:, j * 128:(j + 1) * 128], qcb[:, ho + j * 128:ho + (j + 1) * 128], idenb[:]))
                        for j in range(cw // 128):
                            fc = n0 // 128 + j
                            dst, dstr = hqT_ap(tth, fc)
                            dep(r=[pstr], w=[dstr])
                            C_("vector", lambda e, j=j, dst=dst, pst=pst: e.tensor_copy(dst, pst[:, j * 128:(j + 1) * 128]))
                        if n0 == CHUNKS[-1][0]:
                            dproj(half, tth)

        OWN = dict(ENG_SEM)

        def emit(e, name):
            clock = dict.fromkeys(SEMS, 0)
            own = OWN.get(name)
            for (eng, fn, sem, val, deps, inc) in PLAN:
                if eng != name:
                    continue
                for s, v in deps.items():
                    if s == own and own == "ct":
                        continue
                    if v > clock[s]:
                        e.wait_ge(SEMOBJ[s], v)
                        clock[s] = v
                fn(e).then_inc(SEMOBJ[sem], inc)
                if sem == own and own == "ct":
                    clock[sem] = val

        @block.sync
        def _(e):
            emit(e, "sync")
            for s in SEMS:
                if cum[s]:
                    e.wait_ge(SEMOBJ[s], cum[s])

        @block.vector
        def _(e):
            emit(e, "vector")

        @block.scalar
        def _(e):
            emit(e, "scalar")

        @block.gpsimd
        def _(e):
            emit(e, "gpsimd")

        @block.tensor
        def _(e):
            emit(e, "tensor")

    _ctx.close()
    return nc


def _host_inputs(inputs, b):
    cc = _build_consts()
    sw = np.asarray(inputs["L0_attn_short_w"], np.float32).reshape(24, 128, 3).transpose(1, 0, 2).reshape(128, 72)
    inb = np.asarray(inputs["L0_attn_in_b"], np.float32).reshape(24, 128).T
    sb = np.asarray(inputs["L0_attn_short_b"], np.float32).reshape(24, 128).T
    fb = np.asarray(inputs["L0_attn_filter_bias"], np.float32).reshape(8, 128).T
    m = {
        "u": np.ascontiguousarray(np.asarray(inputs["L0_attn_u"], np.float32)[b]),
        "c_row": np.ascontiguousarray(np.asarray(inputs["L0_c"], np.float32)[b:b + 1]),
        "in_w": np.asarray(inputs["L0_attn_in_w"], np.float32),
        "swp": np.ascontiguousarray(sw), "inbp": np.ascontiguousarray(inb),
        "sbp": np.ascontiguousarray(sb), "fbp": np.ascontiguousarray(fb),
        "mlp_w0": np.asarray(inputs["L0_attn_mlp_w0"], np.float32),
        "mlp_b0": np.ascontiguousarray(np.asarray(inputs["L0_attn_mlp_b0"], np.float32).reshape(-1)[:, None]),
        "freq0": np.ascontiguousarray(np.asarray(inputs["L0_attn_freq0"], np.float32).reshape(-1)[:, None]),
        "mlp_w1": np.asarray(inputs["L0_attn_mlp_w1"], np.float32),
        "mlp_b1": np.ascontiguousarray(np.asarray(inputs["L0_attn_mlp_b1"], np.float32).reshape(-1)[:, None]),
        "freq1": np.ascontiguousarray(np.asarray(inputs["L0_attn_freq1"], np.float32).reshape(-1)[:, None]),
        "mlp_w2": np.asarray(inputs["L0_attn_mlp_w2"], np.float32),
        "mlp_b2": np.ascontiguousarray(np.asarray(inputs["L0_attn_mlp_b2"], np.float32).reshape(-1)[:, None]),
        "freq2": np.ascontiguousarray(np.asarray(inputs["L0_attn_freq2"], np.float32).reshape(-1)[:, None]),
        "mlp_w3": np.asarray(inputs["L0_attn_mlp_w3"], np.float32),
        "out_w": np.asarray(inputs["L0_attn_out_w"], np.float32),
        "out_b": np.asarray(inputs["L0_attn_out_b"], np.float32)[None, :],
        "w_gate": np.asarray(inputs["L0_ffn_w_gate"], np.float32),
        "w_up": np.asarray(inputs["L0_ffn_w_up"], np.float32),
        "w_down": np.asarray(inputs["L0_ffn_w_down"], np.float32),
        "wmod": np.asarray(inputs["L0_wmod"], np.float32),
    }
    m.update(cc)
    return m


def kernel(**inputs):
    nc = build_program()
    in_maps = [_host_inputs(inputs, b) for b in range(8)]
    res = run_bass_kernel_spmd(nc, in_maps, core_ids=list(range(8)))
    out = np.stack([np.asarray(res.results[b]["out"], np.float32) for b in range(8)], axis=0)
    return out


# version 12
# speedup vs baseline: 1.0053x; 1.0053x over previous
import numpy as np
import ml_dtypes
import concourse.bass as bass
import concourse.mybir as mybir
from concourse.bass_utils import run_bass_kernel_spmd

F32 = mybir.dt.float32
BF16 = mybir.dt.bfloat16
I32 = mybir.dt.int32
AF = mybir.ActivationFunctionType
ALU = mybir.AluOpType
AX = mybir.AxisListType
bfloat16 = ml_dtypes.bfloat16

L = 2048
D = 1024
TW = 3072
FI = 2816
NT = 16
NG = 8
NKC = 8
NFC = 22

_CONSTS = None

def _build_consts():
    global _CONSTS
    if _CONSTS is not None:
        return _CONSTS
    S1C = np.zeros((128, 264))
    for q in range(4):
        for n1 in range(32):
            p = q * 32 + n1
            for a in range(33):
                ang = 2 * np.pi * n1 * a / 64.0
                S1C[p, q * 66 + a * 2 + 0] = np.cos(ang)
                S1C[p, q * 66 + a * 2 + 1] = -np.sin(ang)
    UH = np.zeros((33, 128, 128))
    for a in range(33):
        for n2 in range(64):
            Tr = np.cos(2 * np.pi * n2 * a / 4096.0)
            Ti = -np.sin(2 * np.pi * n2 * a / 4096.0)
            for g in range(64):
                wr = np.cos(2 * np.pi * n2 * g / 64.0)
                wi = -np.sin(2 * np.pi * n2 * g / 64.0)
                UH[a, 0 * 64 + n2, 0 * 64 + g] = Tr * wr - Ti * wi
                UH[a, 1 * 64 + n2, 0 * 64 + g] = -(Tr * wi + Ti * wr)
                UH[a, 0 * 64 + n2, 1 * 64 + g] = Tr * wi + Ti * wr
                UH[a, 1 * 64 + n2, 1 * 64 + g] = Tr * wr - Ti * wi
    UHs = np.concatenate([UH[a] for a in range(33)], axis=1)
    W1 = np.zeros((128, 128))
    for g in range(64):
        for n2o in range(64):
            c = np.cos(2 * np.pi * n2o * g / 64.0)
            s = np.sin(2 * np.pi * n2o * g / 64.0)
            W1[0 * 64 + g, n2o * 2 + 0] = c
            W1[1 * 64 + g, n2o * 2 + 0] = -s
            W1[0 * 64 + g, n2o * 2 + 1] = s
            W1[1 * 64 + g, n2o * 2 + 1] = c
    W2 = np.zeros((128, 64 * 32))
    for n2o in range(64):
        for a in range(33):
            rho = 1.0 if (a == 0 or a == 32) else 2.0
            for m in range(32):
                t = 64 * m + n2o
                ang = 2 * np.pi * t * a / 4096.0
                W2[a * 2 + 0, n2o * 32 + m] = rho * np.cos(ang) / 4096.0
                W2[a * 2 + 1, n2o * 32 + m] = -rho * np.sin(ang) / 4096.0
    t_hat = np.linspace(0.0, 1.0, L)
    w = 2.0 * np.pi * np.linspace(0.0, L - 1.0, L) / L
    phase = 1e-4 * w
    zT = np.stack([t_hat, np.cos(phase), -np.sin(phase)], axis=0)
    MAX_DECAY = np.log(0.01) / 0.3
    MIN_DECAY = np.log(0.01) / 1.5
    deltas = np.abs(np.linspace(MIN_DECAY, MAX_DECAY, 1024))
    _CONSTS = {
        "cc_s1c": S1C.astype(np.float32),
        "cc_uhat": UHs.astype(np.float32),
        "cc_w1": W1.astype(np.float32),
        "cc_w2": W2.astype(np.float32),
        "cc_zt": np.ascontiguousarray(zT.astype(np.float32)),
        "cc_adelta": np.ascontiguousarray(deltas.reshape(8, 128).T.astype(np.float32)),
        "cc_that": t_hat[None, :].astype(np.float32),
        "cc_iden": np.eye(128, dtype=np.float32),
    }
    return _CONSTS

INPUT_SHAPES = {
    "u": (L, D), "c_row": (1, D),
    "in_w": (D, TW), "swp": (128, 72), "inbp": (128, 24), "sbp": (128, 24),
    "mlp_w0": (3, 64), "mlp_b0": (64, 1), "freq0": (64, 1),
    "mlp_w1": (64, 64), "mlp_b1": (64, 1), "freq1": (64, 1),
    "mlp_w2": (64, 64), "mlp_b2": (64, 1), "freq2": (64, 1),
    "mlp_w3": (64, D), "fbp": (128, 8),
    "out_w": (D, D), "out_b": (1, D),
    "w_gate": (FI, D), "w_up": (FI, D), "w_down": (D, FI),
    "wmod": (D, 6 * D),
    "cc_s1c": (128, 264), "cc_uhat": (128, 33 * 128), "cc_w1": (128, 128),
    "cc_w2": (128, 64 * 32), "cc_zt": (3, L), "cc_adelta": (128, 8),
    "cc_that": (1, L), "cc_iden": (128, 128),
}

BIG = 1 << 60


def build_program(dbg=(), phase_limit=99):
    nc = bass.Bass("TRN2", target_bir_lowering=False)
    IN = {}
    for name, shp in INPUT_SHAPES.items():
        IN[name] = nc.dram_tensor(name, list(shp), F32, kind="ExternalInput").ap()
    out_d = nc.dram_tensor("out", [L, D], F32, kind="ExternalOutput").ap()
    wgqT_d = nc.dram_tensor("wgqT", [D, FI], BF16, kind="Internal").ap()
    wuqT_d = nc.dram_tensor("wuqT", [D, FI], BF16, kind="Internal").ap()
    wdqT_d = nc.dram_tensor("wdqT", [FI, D], BF16, kind="Internal").ap()
    modrow_d = nc.dram_tensor("modrow_d", [1, 6 * D], BF16, kind="Internal").ap()
    w2b16_d = nc.dram_tensor("w2b16_d", [128, 2048], BF16, kind="Internal").ap()

    SEMS = ["cv", "cs", "ct", "cp", "dS0", "dS1", "dS2", "dG0", "dG1", "dG2"]
    ENG_SEM = {"vector": "cv", "scalar": "cs", "tensor": "ct", "gpsimd": "cp"}
    QUEUE_ROT = {"sync": ["dS0", "dS1", "dS2"], "gpsimd": ["dG0", "dG1", "dG2"]}
    QCNT = {"sync": 0, "gpsimd": 0}
    cum = dict.fromkeys(SEMS, 0)
    resW = {}
    resR = {}
    floor = {}
    PEND = {"on": False, "r": [], "w": []}
    PLAN = []

    def _ivl(spec):
        if isinstance(spec, str):
            return (spec, 0, BIG)
        return spec

    def dep(r=(), w=()):
        PEND["r"] = [_ivl(x) for x in r]
        PEND["w"] = [_ivl(x) for x in w]
        PEND["on"] = True

    def st(eng, fn, kind):
        if kind == "dma":
            rot = QUEUE_ROT[eng]
            sem = rot[QCNT[eng] % len(rot)]
            QCNT[eng] += 1
        else:
            sem = ENG_SEM[eng]
        inc = 16 if kind == "dma" else 1
        deps = dict(floor)
        if kind == "dma" and cum[sem] > deps.get(sem, 0):
            deps[sem] = cum[sem]
        if PEND["on"]:
            rs, ws = PEND["r"], PEND["w"]
            PEND["on"] = False
            annotated = True
            for (n, lo, hi) in rs:
                for (l2, h2, s2, v2) in resW.get(n, ()):
                    if l2 < hi and lo < h2:
                        if deps.get(s2, 0) < v2:
                            deps[s2] = v2
            for (n, lo, hi) in ws:
                for lst in (resW, resR):
                    for (l2, h2, s2, v2) in lst.get(n, ()):
                        if l2 < hi and lo < h2:
                            if deps.get(s2, 0) < v2:
                                deps[s2] = v2
        else:
            annotated = False
            for s2 in SEMS:
                if cum[s2] > deps.get(s2, 0):
                    deps[s2] = cum[s2]
        val = cum[sem] + inc
        cum[sem] = val
        if annotated:
            for (n, lo, hi) in rs:
                resR.setdefault(n, []).append((lo, hi, sem, val))
            for (n, lo, hi) in ws:
                for lst in (resW, resR):
                    if n in lst:
                        lst[n] = [e for e in lst[n] if not (lo <= e[0] and e[1] <= hi)]
                resW.setdefault(n, []).append((lo, hi, sem, val))
        else:
            floor.clear()
            floor[sem] = val
            for k in list(resW):
                resW[k] = []
            for k in list(resR):
                resR[k] = []
        PLAN.append((eng, fn, sem, val, deps, inc))

    def D_(eng, out_ap, in_ap, **kw):
        def f(e, o=out_ap, i=in_ap, kw=kw):
            return e.dma_start(out=o, in_=i, **kw)
        st(eng, f, "dma")

    def C_(eng, fn):
        st(eng, fn, "compute")

    def DBG(name, ap, shape, dtype=F32, r=()):
        if name in dbg:
            d = nc.dram_tensor("dbg_" + name, list(shape), dtype, kind="ExternalOutput").ap()
            if r:
                dep(r=r)
            D_("sync", d, ap)

    from contextlib import ExitStack
    _ctx = ExitStack()
    E = _ctx.enter_context
    xres = E(nc.sbuf_tensor("xres", [128, NT, D], BF16))
    hT = E(nc.sbuf_tensor("hT", [128, NKC, L], BF16))
    ycube = E(nc.sbuf_tensor("ycube", [128, NG * L], BF16))
    big1 = E(nc.sbuf_tensor("big1", [128, 12320], BF16))
    big2 = E(nc.sbuf_tensor("big2", [128, 22912], BF16))
    big3 = E(nc.sbuf_tensor("big3", [128, 4096], BF16))
    qa = E(nc.sbuf_tensor("qa", [128, 1024], F32))
    qca = E(nc.sbuf_tensor("qca", [128, 1024], BF16))
    qcb = E(nc.sbuf_tensor("qcb", [128, 1024], BF16))
    qcc = E(nc.sbuf_tensor("qcc", [128, 1024], BF16))
    s1cb = E(nc.sbuf_tensor("s1cb", [128, 264], BF16))
    uhatb = E(nc.sbuf_tensor("uhatb", [128, 33 * 128], BF16))
    w1b = E(nc.sbuf_tensor("w1b", [128, 128], BF16))
    idenb = E(nc.sbuf_tensor("idenb", [128, 128], BF16))
    w3b = E(nc.sbuf_tensor("w3b", [64, D], BF16))
    h3 = E(nc.sbuf_tensor("h3", [64, L], BF16))
    mlpw = E(nc.sbuf_tensor("mlpw", [64, 192], F32))
    mlpc = E(nc.sbuf_tensor("mlpc", [64, 8], F32))
    swt = E(nc.sbuf_tensor("swt", [128, 24, 3], F32))
    inbt = E(nc.sbuf_tensor("inbt", [128, 24], F32))
    bprime = E(nc.sbuf_tensor("bprime", [128, 24], F32))
    fbt = E(nc.sbuf_tensor("fbt", [128, 8], F32))
    acol = E(nc.sbuf_tensor("acol", [128, 8], F32))
    reps = E(nc.sbuf_tensor("reps", [128, 2, D], BF16))
    onesrow = E(nc.sbuf_tensor("onesrow", [1, 128], BF16))
    lns = E(nc.sbuf_tensor("lns", [128, 16], F32))
    qm = E(nc.sbuf_tensor("qm", [128, 64], F32))
    qr = E(nc.sbuf_tensor("qr", [128, 64], F32))
    qsb = E(nc.sbuf_tensor("qsb", [128, 64], BF16))
    ps_a = E(nc.psum_tensor("ps_a", [128, 512], F32))
    ps_b = E(nc.psum_tensor("ps_b", [128, 512], F32))
    ps_w = E(nc.psum_tensor("ps_w", [128, 1024], F32))
    ps_x = E(nc.psum_tensor("ps_x", [128, 1024], F32))
    ps_t1 = E(nc.psum_tensor("ps_t1", [128, 1024], BF16))
    ps_t2 = E(nc.psum_tensor("ps_t2", [128, 1024], BF16))
    SEMOBJ = {s: E(nc.semaphore("sem_" + s)) for s in SEMS}
    block = E(nc.Block())
    if True:
        B1_PAD = [0, 2052, 4104]
        B1_V1 = 6156
        B1_DEC = 8204
        B1_LN = 10252
        B1_H = 11276
        B2_X1 = 0
        B2_VIN = 2048
        B2_X0 = 4096
        B2_SA = 6144
        B2_SB = 10368
        B2_K2 = 14592
        B2_U = B2_SA

        def b2(off, ln):
            return big2[:, off:off + ln]

        def b1(off, ln):
            return big1[:, off:off + ln]

        def r1(off, ln):
            return ("big1", off, off + ln)

        def r2(off, ln):
            return ("big2", off, off + ln)

        def r3(off, ln):
            return ("big3", off, off + ln)

        RYW = lambda lo, hi: ("ycube", lo, hi)

        big1f = big1[:].bitcast(F32)
        ycubef = ycube[:].bitcast(F32)
        big2f = big2[:].bitcast(F32)
        big3f = big3[:].bitcast(F32)
        thatrep = big2f[:, 9408:9408 + 2048]
        R_THAT = r2(18816, 22912)
        modrow = big1[0:1, 0:6144]
        AFC = AF.Copy

        def quant_block(src_ap, dst_ap, W, absf, src_r, dst_r, abs_r, half=None):
            Bn = W // 16
            if half is None:
                qo, bo = 0, 0
            else:
                qo, bo = half * 512, half * 32
            Rq = lambda nm: (nm, bo, bo + Bn)
            Rb = lambda nm: (nm, qo, qo + W)
            qmv = qm[:, bo:bo + Bn]
            qrv = qr[:, bo:bo + Bn]
            qsbv = qsb[:, bo:bo + Bn]
            win = lambda t: t.rearrange("p (nb w) -> p nb w", w=16)
            dep(r=[src_r], w=[Rq("qm")])
            C_("vector", lambda e: e.reduce_max(qmv.unsqueeze(2), win(src_ap), axis=AX.X, apply_absolute_value=True))
            dep(r=[Rq("qm")], w=[Rq("qm")])
            C_("vector", lambda e: e.tensor_scalar_max(qmv, qmv, 6e-12))
            dep(r=[Rq("qm")], w=[Rq("qr")])
            C_("vector", lambda e: e.reciprocal(qrv, qmv))
            dep(r=[Rq("qm")], w=[Rq("qsb")])
            C_("vector", lambda e: e.tensor_scalar(qsbv, qmv, 1.0 / 12.0, None, ALU.mult))
            dep(r=[src_r], w=[abs_r])
            C_("scalar", lambda e: e.activation(absf[:, 0:W], src_ap, AF.Abs))
            abs_res = abs_r
            rbc = qrv.unsqueeze(2).broadcast_to([128, Bn, 16])
            dep(r=[Rq("qr"), abs_res], w=[abs_res])
            C_("vector", lambda e: e.scalar_tensor_tensor(win(absf[:, 0:W]), win(absf[:, 0:W]), 6.0, rbc, ALU.mult, ALU.mult))
            u = absf[:, 0:W]
            A, Bf, Cf = qca[:, qo:qo + W], qcb[:, qo:qo + W], qcc[:, qo:qo + W]
            dep(r=[abs_res], w=[Rb("qca")])
            C_("vector", lambda e: e.tensor_single_scalar(A, u, 0.25, ALU.is_ge))
            for th in (0.75, 1.25, 1.75):
                dep(r=[abs_res, Rb("qca")], w=[Rb("qca")])
                C_("vector", lambda e, th=th: e.scalar_tensor_tensor(A, u, th, A, ALU.is_ge, ALU.add))
            dep(r=[abs_res], w=[Rb("qcb")])
            C_("vector", lambda e: e.tensor_single_scalar(Bf, u, 5.0, ALU.is_ge))
            for th in (5.0, 3.5, 2.5):
                dep(r=[abs_res, Rb("qcb")], w=[Rb("qcb")])
                C_("vector", lambda e, th=th: e.scalar_tensor_tensor(Bf, u, th, Bf, ALU.is_ge, ALU.add))
            dep(r=[Rb("qca"), Rb("qcb")], w=[Rb("qca")])
            C_("vector", lambda e: e.scalar_tensor_tensor(A, Bf, 2.0, A, ALU.mult, ALU.add))
            dep(r=[src_r], w=[Rb("qcb")])
            C_("scalar", lambda e: e.activation(Bf, src_ap, AF.Sign))
            sbc = qsbv.unsqueeze(2).broadcast_to([128, Bn, 16])
            dep(r=[Rb("qcb"), Rq("qsb")], w=[Rb("qcc")])
            C_("gpsimd", lambda e: e.tensor_tensor(win(Cf), win(Bf), sbc, ALU.mult))
            dep(r=[Rb("qca"), Rb("qcc")], w=[dst_r])
            C_("gpsimd", lambda e: e.tensor_mul(dst_ap, A, Cf))

        def load_reps(slots):
            for (j, ri) in slots:
                dep(r=["modrowd"], w=["qcb"])
                D_("sync", qcb[0:1, 0:D], modrow_d[0:1, j * D:(j + 1) * D])
                for hh in range(2):
                    dep(r=["qcb", "onesrow"], w=["ps_w"])
                    C_("tensor", lambda e, hh=hh: e.matmul(ps_w[:, hh * 512:(hh + 1) * 512], onesrow[:], qcb[0:1, hh * 512:(hh + 1) * 512], start=True, stop=True))
                dep(r=["ps_w"], w=[("reps", ri, ri + 1)])
                C_("scalar", lambda e, ri=ri: e.activation(reps[:, ri, :], ps_w[:], AFC))

        dep(w=["xres"])
        D_("gpsimd", xres[:], IN["u"].rearrange("(T p) d -> p T d", p=128))
        dep(w=[r3(0, 256)])
        D_("sync", big3f[:, 0:128], IN["cc_iden"])
        dep(r=[r3(0, 256)], w=["idenb"])
        C_("vector", lambda e: e.tensor_copy(idenb[:], big3f[:, 0:128]))
        for nm, t, rn in (("cc_s1c", s1cb, "s1cb"), ("cc_uhat", uhatb, "uhatb"), ("cc_w1", w1b, "w1b")):
            dep(w=[rn])
            D_("gpsimd", t[:], IN[nm])
        dep(w=[r2(0, 4096)])
        D_("sync", big2f[:, 0:2048], IN["cc_w2"])
        dep(r=[r2(0, 4096)], w=[r2(8192, 10240)])
        C_("vector", lambda e: e.tensor_copy(big2[:, 8192:10240], big2f[:, 0:2048]))
        dep(r=[r2(8192, 10240)], w=["w2b16d"])
        D_("sync", w2b16_d, big2[:, 8192:10240])
        dep(w=["w3b"])
        D_("gpsimd", w3b[:], IN["mlp_w3"])
        dep(w=["swt"])
        D_("sync", swt[:], IN["swp"].rearrange("p (cc k) -> p cc k", k=3))
        dep(w=["inbt"])
        D_("sync", inbt[:], IN["inbp"])
        dep(w=["fbt"])
        D_("sync", fbt[:], IN["fbp"])
        dep(w=["acol"])
        D_("sync", acol[:], IN["cc_adelta"])
        dep(r=["acol"], w=["acol"])
        C_("vector", lambda e: e.tensor_scalar_mul(acol[:], acol[:], -1.0))
        dep(w=["onesrow"])
        C_("vector", lambda e: e.memset(onesrow[:], 1.0))
        dep(w=["bprime"])
        D_("sync", bprime[:], IN["sbp"])
        dep(r=["swt"], w=["qm"])
        C_("vector", lambda e: e.reduce_sum(qm[:, 0:24].unsqueeze(2), swt[:], axis=AX.X))
        dep(r=["inbt", "qm"], w=["qr"])
        C_("vector", lambda e: e.tensor_mul(qr[:, 0:24], inbt[:], qm[:, 0:24]))
        dep(r=["qr", "bprime"], w=["bprime"])
        C_("vector", lambda e: e.tensor_add(bprime[:], bprime[:], qr[:, 0:24]))
        dep(w=[R_THAT])
        D_("sync", thatrep, IN["cc_that"].partition_broadcast(128))

        if phase_limit >= 2:
            scf = big1f[0:1, 3072:4096]
            dep(w=[r1(6144, 8192)])
            D_("sync", scf, IN["c_row"])
            dep(r=[r1(6144, 8192)], w=[r1(8192, 10240)])
            C_("scalar", lambda e: e.activation(big1f[0:1, 4096:5120], scf, AF.Silu))
            dep(r=[r1(8192, 10240)], w=["qcb"])
            C_("vector", lambda e: e.tensor_copy(qcb[0:1, 0:D], big1f[0:1, 4096:5120]))
            for j in range(8):
                dep(r=["qcb", "idenb"], w=["ps_t1"])
                C_("tensor", lambda e, j=j: e.transpose(ps_t1[:, 0:128], qcb[0:1, j * 128:(j + 1) * 128], idenb[0:1, :]))
                dep(r=["ps_t1"], w=["qsb"])
                C_("vector", lambda e, j=j: e.tensor_copy(qsb[:, j:j + 1], ps_t1[:, 0:1]))
            for csk in range(12):
                dep(w=[r3(0, 4096)])
                D_("gpsimd", big3[:, 0:NKC * 512].rearrange("p (kc n) -> p kc n", n=512),
                   IN["wmod"].rearrange("(kc p) n -> p kc n", p=128)[:, :, csk * 512:(csk + 1) * 512])
                for kc in range(NKC):
                    dep(r=["qsb", r3(0, 4096)], w=["ps_a"])
                    C_("tensor", lambda e, kc=kc: e.matmul(
                        ps_a[0:1, 0:512], qsb[:, kc:kc + 1],
                        big3[:, kc * 512:(kc + 1) * 512], start=(kc == 0), stop=(kc == 7)))
                dep(r=["ps_a"], w=[r1(csk * 512, (csk + 1) * 512)])
                C_("scalar", lambda e, csk=csk: e.activation(modrow[:, csk * 512:(csk + 1) * 512], ps_a[0:1, 0:512], AFC))
            for j in (2, 5):
                g_ap = modrow[:, j * D:(j + 1) * D]
                dep(r=[r1(j * D, (j + 1) * D)], w=[r1(8192, 9216)])
                C_("scalar", lambda e, g=g_ap: e.activation(b1(8192, D)[0:1, :], g, AF.Abs))
                dep(r=[r1(8192, 9216)], w=[r1(8192, 9216)])
                C_("vector", lambda e: e.tensor_scalar_max(b1(8192, D)[0:1, :], b1(8192, D)[0:1, :], 0.25))
                dep(r=[r1(j * D, (j + 1) * D)], w=[r1(9216, 10240)])
                C_("scalar", lambda e, g=g_ap: e.activation(b1(9216, D)[0:1, :], g, AF.Sign))
                dep(r=[r1(8192, 10240)], w=[r1(j * D, (j + 1) * D)])
                C_("vector", lambda e, g=g_ap: e.tensor_mul(g, b1(8192, D)[0:1, :], b1(9216, D)[0:1, :]))
            for j in (1, 4):
                g_ap = modrow[:, j * D:(j + 1) * D]
                dep(r=[r1(j * D, (j + 1) * D)], w=[r1(j * D, (j + 1) * D)])
                C_("vector", lambda e, g=g_ap: e.tensor_scalar_add(g, g, 1.0))
            DBG("modrow", modrow, (1, 6 * D), BF16, r=[r1(0, 6144)])
            dep(r=[r1(0, 6144)], w=["modrowd"])
            D_("sync", modrow_d, modrow)

            for jj, (bn, fn_) in enumerate((("mlp_b0", "freq0"), ("mlp_b1", "freq1"), ("mlp_b2", "freq2"))):
                dep(w=[("mlpc", 6, 7)])
                D_("sync", mlpc[:, 6:7], IN[bn])
                dep(w=[("mlpc", 2 * jj + 1, 2 * jj + 2)])
                D_("sync", mlpc[:, 2 * jj + 1:2 * jj + 2], IN[fn_])
                dep(r=[("mlpc", 6, 7), ("mlpc", 2 * jj + 1, 2 * jj + 2)], w=[("mlpc", 2 * jj, 2 * jj + 1)])
                C_("vector", lambda e, jj=jj: e.tensor_mul(mlpc[:, 2 * jj:2 * jj + 1], mlpc[:, 6:7], mlpc[:, 2 * jj + 1:2 * jj + 2]))
            dep(w=[("mlpw", 0, 64)])
            D_("sync", mlpw[:, 0:64], IN["mlp_w1"])
            dep(w=[("mlpw", 64, 128)])
            D_("sync", mlpw[:, 64:128], IN["mlp_w2"])
            dep(w=[("mlpw", 128, 192)])
            D_("sync", mlpw[0:3, 128:192], IN["mlp_w0"])
            ztb = big3f[0:3, 0:L]
            dep(w=[r3(0, 4096)])
            D_("sync", ztb, IN["cc_zt"])
            fsc = big1f[0:64, 0:2048]
            qus = big1f[0:64, 2048:4096]
            sinf = big1f[0:64, 4096:6144]
            qai = qa[:].bitcast(I32)
            RFSC = r1(0, 4096)
            RQUS = r1(4096, 8192)
            RSINF = r1(8192, 12288)

            def sin_layer(ci, outf, outr):
                dep(r=[RFSC, "mlpc"], w=[RFSC])
                C_("vector", lambda e, ci=ci: e.tensor_scalar(fsc, fsc, mlpc[0:64, 2 * ci + 1:2 * ci + 2], mlpc[0:64, 2 * ci:2 * ci + 1], ALU.mult, ALU.add))
                dep(r=[RFSC], w=[RQUS])
                C_("vector", lambda e: e.tensor_scalar_mul(qus, fsc, float(1.0 / (2 * np.pi))))
                for hh in range(2):
                    dep(r=[RQUS], w=["qa"])
                    C_("vector", lambda e, hh=hh: e.tensor_copy(qai[0:64, 0:1024], qus[:, hh * 1024:(hh + 1) * 1024]))
                    dep(r=["qa"], w=[RQUS])
                    C_("vector", lambda e, hh=hh: e.tensor_copy(qus[:, hh * 1024:(hh + 1) * 1024], qai[0:64, 0:1024]))
                dep(r=[RQUS, RFSC], w=[RQUS])
                C_("vector", lambda e: e.scalar_tensor_tensor(qus, qus, float(-2 * np.pi), fsc, ALU.mult, ALU.add))
                dep(r=[RQUS], w=[outr])
                C_("scalar", lambda e, outf=outf: e.activation(outf, qus, AF.Sin))

            for ck in range(4):
                dep(r=["mlpw", r3(0, 4096)], w=["ps_b"])
                C_("tensor", lambda e, ck=ck: e.matmul(ps_b[0:64, 0:512], mlpw[0:3, 128:192], ztb[:, ck * 512:(ck + 1) * 512], start=True, stop=True))
                dep(r=["ps_b"], w=[RFSC])
                C_("scalar", lambda e, ck=ck: e.activation(fsc[:, ck * 512:(ck + 1) * 512], ps_b[0:64, 0:512], AFC))
            DBG("fl1", fsc, (64, L), r=[RFSC])
            sin_layer(0, sinf, RSINF)
            DBG("fs0", sinf, (64, L), r=[RSINF])
            for li in range(2):
                for ck in range(4):
                    dep(r=["mlpw", RSINF], w=["ps_b"])
                    C_("tensor", lambda e, li=li, ck=ck: e.matmul(ps_b[0:64, 0:512], mlpw[0:64, li * 64:(li + 1) * 64], sinf[:, ck * 512:(ck + 1) * 512], start=True, stop=True))
                    dep(r=["ps_b"], w=[RFSC])
                    C_("scalar", lambda e, ck=ck: e.activation(fsc[:, ck * 512:(ck + 1) * 512], ps_b[0:64, 0:512], AFC))
                sin_layer(li + 1, sinf if li == 0 else h3[0:64, :], RSINF if li == 0 else "h3")
            DBG("h3", h3[0:64, :], (64, L), BF16, r=["h3"])

        def ln_stats(xt, tt):
            rx = ("xres", tt, tt + 1)
            dep(r=[rx], w=[r1(B1_LN, B1_LN + D), "lns"])
            C_("scalar", lambda e, xt=xt: e.activation(b1(B1_LN, D), xt, AF.Square, accum_out=lns[:, 0:1]))
            dep(r=[rx, "lns"], w=["lns"])
            C_("vector", lambda e, xt=xt: e.reduce_sum(lns[:, 1:2], xt, axis=AX.X))
            dep(r=["lns"], w=["lns"])
            C_("vector", lambda e: e.tensor_scalar_mul(lns[:, 2:3], lns[:, 1:2], 1.0 / D))
            dep(r=["lns"], w=["lns"])
            C_("vector", lambda e: e.tensor_scalar_mul(lns[:, 3:4], lns[:, 0:1], 1.0 / D))
            dep(r=["lns"], w=["lns"])
            C_("vector", lambda e: e.tensor_mul(lns[:, 4:5], lns[:, 2:3], lns[:, 2:3]))
            dep(r=["lns"], w=["lns"])
            C_("vector", lambda e: e.tensor_sub(lns[:, 5:6], lns[:, 3:4], lns[:, 4:5]))
            dep(r=["lns"], w=["lns"])
            C_("vector", lambda e: e.tensor_scalar_add(lns[:, 5:6], lns[:, 5:6], 1e-5))
            dep(r=["lns"], w=["lns"])
            C_("scalar", lambda e: e.activation(lns[:, 6:7], lns[:, 5:6], AF.Sqrt))
            dep(r=["lns"], w=["lns"])
            C_("vector", lambda e: e.reciprocal(lns[:, 7:8], lns[:, 6:7]))
            dep(r=["lns"], w=["lns"])
            C_("vector", lambda e: e.tensor_mul(lns[:, 8:9], lns[:, 2:3], lns[:, 7:8]))
            dep(r=["lns"], w=["lns"])
            C_("vector", lambda e: e.tensor_scalar_mul(lns[:, 8:9], lns[:, 8:9], -1.0))

        if phase_limit >= 3:
            load_reps([(1, 0), (0, 1)])
            for tt in range(NT):
                xt = xres[:, tt, :]
                rx = ("xres", tt, tt + 1)
                rht = ("hT", tt * 128, (tt + 1) * 128)
                ln_stats(xt, tt)
                dep(r=[rx, "lns"], w=[r1(B1_H, B1_H + D)])
                C_("scalar", lambda e, xt=xt: e.activation(b1(B1_H, D), xt, AF.Identity, bias=lns[:, 8:9], scale=lns[:, 7:8]))
                dep(r=[r1(B1_H, B1_H + D), ("reps", 0, 1)], w=[r1(B1_H, B1_H + D)])
                C_("vector", lambda e: e.tensor_mul(b1(B1_H, D), b1(B1_H, D), reps[:, 0, :]))
                dep(r=[r1(B1_H, B1_H + D), ("reps", 1, 2)], w=[r1(B1_H, B1_H + D)])
                C_("vector", lambda e: e.tensor_add(b1(B1_H, D), b1(B1_H, D), reps[:, 1, :]))
                for kc in range(NKC):
                    dep(r=[r1(B1_H, B1_H + D), "idenb"], w=["ps_t2"])
                    C_("tensor", lambda e, kc=kc: e.transpose(ps_t2[:, kc * 128:(kc + 1) * 128], b1(B1_H + kc * 128, 128), idenb[:]))
                dep(r=["ps_t2"], w=[rht])
                C_("scalar", lambda e, tt=tt: e.activation(hT[:, :, tt * 128:(tt + 1) * 128], ps_t2[:].rearrange("p (kc n) -> p kc n", n=128), AFC))
            DBG("hT", hT[:].rearrange("p kc t -> p (kc t)"), (128, NKC * L), BF16, r=["hT"])

        WQ0 = 10240
        wq_emitters = []

        def _emit_gu_slice(wname, dst, dstr, sl):
            def em():
                wsrc = ycubef[:, 4096:4096 + D]
                dep(w=[RYW(8192, 10240)])
                D_("sync", wsrc, IN[wname][sl * 128:(sl + 1) * 128, :])
                quant_block(wsrc, ycube[:, WQ0:WQ0 + D], D, qa, RYW(8192, 10240), RYW(WQ0, WQ0 + D), "qa")
                for kc in range(NKC):
                    dep(r=[RYW(WQ0 + kc * 128, WQ0 + (kc + 1) * 128), "idenb"], w=["ps_t2"])
                    C_("tensor", lambda e, kc=kc: e.transpose(ps_t2[:, kc * 128:(kc + 1) * 128], ycube[:, WQ0 + kc * 128:WQ0 + (kc + 1) * 128], idenb[:]))
                dep(r=["ps_t2"], w=[RYW(13056, 14080)])
                C_("vector", lambda e: e.tensor_copy(ycube[:, 13056:14080], ps_t2[:]))
                dep(r=[RYW(13056, 14080)], w=[(dstr, sl, sl + 1)])
                D_("sync", dst.rearrange("(kc p) n -> p kc n", p=128)[:, :, sl * 128:(sl + 1) * 128],
                   ycube[:, 13056:14080].rearrange("p (kc n) -> p kc n", n=128))
            return em

        def _emit_wd_slice(sl):
            def em():
                for pc, pw in ((0, 1024), (1024, 1024), (2048, 768)):
                    dep(w=[RYW(8192, 10240)])
                    D_("sync", ycubef[:, 4096:4096 + pw], IN["w_down"][sl * 128:(sl + 1) * 128, pc:pc + pw])
                    quant_block(ycubef[:, 4096:4096 + pw], ycube[:, WQ0 + pc:WQ0 + pc + pw], pw, qa, RYW(8192, 10240), RYW(WQ0 + pc, WQ0 + pc + pw), "qa")
                for fc in range(NFC):
                    dep(r=[RYW(WQ0 + fc * 128, WQ0 + (fc + 1) * 128), "idenb"], w=["ps_t2"])
                    C_("tensor", lambda e, fc=fc: e.transpose(ps_t2[:, (fc % 8) * 128:(fc % 8 + 1) * 128], ycube[:, WQ0 + fc * 128:WQ0 + (fc + 1) * 128], idenb[:]))
                    if fc % 8 == 7 or fc == NFC - 1:
                        n0 = (fc // 8) * 8
                        cnt = fc - n0 + 1
                        dep(r=["ps_t2"], w=[RYW(13056 + n0 * 128, 13056 + (n0 + cnt) * 128)])
                        C_("vector", lambda e, n0=n0, cnt=cnt: e.tensor_copy(
                            ycube[:, 13056 + n0 * 128:13056 + (n0 + cnt) * 128], ps_t2[:, 0:cnt * 128]))
                dep(r=[RYW(13056, 13056 + NFC * 128)], w=[("wdqT", sl, sl + 1)])
                D_("sync", wdqT_d.rearrange("(fc p) n -> p fc n", p=128)[:, :, sl * 128:(sl + 1) * 128],
                   ycube[:, 13056:13056 + NFC * 128].rearrange("p (fc n) -> p fc n", n=128))
            return em

        if phase_limit >= 1:
            for (wname, dst, dstr) in (("w_gate", wgqT_d, "wgqT"), ("w_up", wuqT_d, "wuqT")):
                for sl in range(NFC):
                    wq_emitters.append(_emit_gu_slice(wname, dst, dstr, sl))
            for sl in range(NKC):
                wq_emitters.append(_emit_wd_slice(sl))

        def drain_wq(k):
            for _ in range(k):
                if wq_emitters:
                    wq_emitters.pop(0)()

        def fwd_fft(src_off, dst_off):
            src = b2(src_off, L)
            rsrc = r2(src_off, src_off + L)
            for Q in range(16):
                pst, pstr = (ps_t1, "ps_t1") if (Q // 8) % 2 == 0 else (ps_t2, "ps_t2")
                for q in range(4):
                    n2 = 4 * Q + q
                    inap = src.rearrange("p (n1 n2) -> p n2 n1", n2=64)[:, n2, :]
                    dep(r=[rsrc, "idenb"], w=[pstr])
                    C_("tensor", lambda e, q=q, Q=Q, inap=inap, pst=pst: e.transpose(
                        pst[32 * q:32 * (q + 1), (Q % 8) * 128:(Q % 8 + 1) * 128], inap, idenb[:],
                        tile_position=(0, 32 * q)))
                if Q % 8 == 7:
                    dep(r=[pstr], w=[r1(B1_V1 + (Q - 7) * 128, B1_V1 + (Q + 1) * 128)])
                    C_("vector", lambda e, Q=Q, pst=pst: e.tensor_copy(
                        big1[:, B1_V1 + (Q - 7) * 128:B1_V1 + (Q + 1) * 128], pst[:]))
            for Q in range(16):
                ps, psr = (ps_a, "ps_a") if Q % 2 == 0 else (ps_b, "ps_b")
                dep(r=[r1(B1_V1 + Q * 128, B1_V1 + (Q + 1) * 128), "s1cb"], w=[psr])
                C_("tensor", lambda e, Q=Q, ps=ps: e.matmul(ps[:, 0:264], big1[:, B1_V1 + Q * 128:B1_V1 + (Q + 1) * 128], s1cb[:], start=True, stop=True))
                Aap = b2(B2_SA, 4224).rearrange("p (a r n2) -> p a r n2", a=33, r=2)[:, :, :, 4 * Q:4 * Q + 4].transpose([0, 3, 1, 2])
                dep(r=[psr], w=[r2(B2_SA, B2_SA + 4224)])
                C_("scalar", lambda e, Aap=Aap, ps=ps: e.activation(Aap, ps[:, 0:264].rearrange("p (q a r) -> p q a r", q=4, a=33), AFC))
            for a in range(33):
                pst, pstr = (ps_t1, "ps_t1") if (a // 8) % 2 == 0 else (ps_t2, "ps_t2")
                inap = b2(B2_SA + a * 128, 128)
                dep(r=[r2(B2_SA + a * 128, B2_SA + (a + 1) * 128), "idenb"], w=[pstr])
                C_("tensor", lambda e, a=a, inap=inap, pst=pst: e.transpose(pst[:, (a % 8) * 128:(a % 8 + 1) * 128], inap, idenb[:]))
                if a % 8 == 7 or a == 32:
                    a0 = (a // 8) * 8
                    cnt = a - a0 + 1
                    dep(r=[pstr], w=[r2(B2_SB + a0 * 128, B2_SB + (a0 + cnt) * 128)])
                    C_("vector", lambda e, a0=a0, cnt=cnt, pst=pst: e.tensor_copy(
                        big2[:, B2_SB + a0 * 128:B2_SB + (a0 + cnt) * 128], pst[:, 0:cnt * 128]))
            for a in range(33):
                ps, psr = (ps_a, "ps_a") if (a // 4) % 2 == 0 else (ps_b, "ps_b")
                dep(r=[r2(B2_SB + a * 128, B2_SB + (a + 1) * 128), "uhatb"], w=[psr])
                C_("tensor", lambda e, a=a, ps=ps: e.matmul(ps[:, (a % 4) * 128:(a % 4 + 1) * 128], big2[:, B2_SB + a * 128:B2_SB + (a + 1) * 128], uhatb[:, a * 128:(a + 1) * 128], start=True, stop=True))
                if a % 4 == 3 or a == 32:
                    a0 = (a // 4) * 4
                    cnt = a - a0 + 1
                    dep(r=[psr], w=[r2(dst_off + a0 * 128, dst_off + (a0 + cnt) * 128)])
                    C_("scalar", lambda e, a0=a0, cnt=cnt, ps=ps: e.activation(b2(dst_off + a0 * 128, cnt * 128), ps[:, 0:cnt * 128], AFC))

        def inv_fft(y_off, g):
            for a in range(33):
                pst, pstr = (ps_t1, "ps_t1") if (a // 8) % 2 == 0 else (ps_t2, "ps_t2")
                inap = b2(y_off + a * 128, 128)
                dep(r=[r2(y_off + a * 128, y_off + (a + 1) * 128), "idenb"], w=[pstr])
                C_("tensor", lambda e, a=a, inap=inap, pst=pst: e.transpose(pst[:, (a % 8) * 128:(a % 8 + 1) * 128], inap, idenb[:]))
                if a % 8 == 7 or a == 32:
                    a0 = (a // 8) * 8
                    cnt = a - a0 + 1
                    dep(r=[pstr], w=[r2(B2_SB + a0 * 128, B2_SB + (a0 + cnt) * 128)])
                    C_("vector", lambda e, a0=a0, cnt=cnt, pst=pst: e.tensor_copy(
                        big2[:, B2_SB + a0 * 128:B2_SB + (a0 + cnt) * 128], pst[:, 0:cnt * 128]))
            for a in range(33):
                ps, psr = (ps_a, "ps_a") if (a // 4) % 2 == 0 else (ps_b, "ps_b")
                dep(r=[r2(B2_SB + a * 128, B2_SB + (a + 1) * 128), "w1b"], w=[psr])
                C_("tensor", lambda e, a=a, ps=ps: e.matmul(ps[:, (a % 4) * 128:(a % 4 + 1) * 128], big2[:, B2_SB + a * 128:B2_SB + (a + 1) * 128], w1b[:], start=True, stop=True))
                if a % 4 == 3 or a == 32:
                    a0 = (a // 4) * 4
                    cnt = a - a0 + 1
                    Dap = b2(B2_K2, 4224).rearrange("p (n2o a r) -> p n2o a r", n2o=64, a=33)[:, :, a0:a0 + cnt, :].transpose([0, 2, 1, 3])
                    dep(r=[psr], w=[r2(B2_K2, B2_K2 + 4224)])
                    C_("scalar", lambda e, Dap=Dap, cnt=cnt, ps=ps: e.activation(Dap, ps[:, 0:cnt * 128].rearrange("p (a4 n2o r) -> p a4 n2o r", n2o=64, r=2), AFC))
            for strip in range(8):
                for j in range(8):
                    n2o = strip * 8 + j
                    inap = b2(B2_K2 + n2o * 66, 66)
                    dep(r=[r2(B2_K2 + n2o * 66, B2_K2 + n2o * 66 + 66), "idenb"], w=["ps_t2"])
                    C_("tensor", lambda e, j=j, inap=inap: e.transpose(ps_t2[0:66, j * 128:(j + 1) * 128], inap, idenb[:]))
                dep(r=["ps_t2"], w=[r1(B1_V1, B1_V1 + 1024)])
                C_("vector", lambda e: e.tensor_copy(big1[0:66, B1_V1:B1_V1 + 1024], ps_t2[0:66, :]))
                for j in range(8):
                    n2o = strip * 8 + j
                    dep(r=[r1(B1_V1, B1_V1 + 1024), r1(B1_DEC, B1_DEC + 2048)], w=["ps_w"])
                    C_("tensor", lambda e, j=j, n2o=n2o: e.matmul(ps_w[:, j * 32:(j + 1) * 32], big1[0:66, B1_V1 + j * 128:B1_V1 + (j + 1) * 128], big1[0:66, B1_DEC + n2o * 32:B1_DEC + (n2o + 1) * 32], start=True, stop=True))
                def tv(t, s):
                    return t.rearrange("p (m n2) -> p n2 m", n2=64)[:, s * 8:(s + 1) * 8, :]
                dep(r=["ps_w", r2(B2_VIN, B2_VIN + L), "fbt"], w=[("ycube", g * L, (g + 1) * L)])
                C_("vector", lambda e, s=strip: e.scalar_tensor_tensor(
                    tv(ycube[:, g * L:(g + 1) * L], s), tv(b2(B2_VIN, L), s), fbt[:, g:g + 1],
                    ps_w[:, 0:256].rearrange("p (j m) -> p j m", j=8), ALU.mult, ALU.add))
            dep(r=[("ycube", g * L, (g + 1) * L), r2(B2_X0, B2_X0 + L)], w=[("ycube", g * L, (g + 1) * L)])
            C_("vector", lambda e: e.tensor_mul(ycube[:, g * L:(g + 1) * L], ycube[:, g * L:(g + 1) * L], b2(B2_X0, L)))

        if phase_limit >= 4:
            inw3 = big3[:, 0:NKC * 384].rearrange("p (kc s n) -> p kc s n", s=3, n=128)

            def inproj(g):
                for s_ in range(3):
                    dep(w=[r3(s_ * 1024, s_ * 1024 + 1024)])
                    D_("gpsimd", inw3[:, :, s_, :],
                       IN["in_w"].rearrange("(kc p) (s nn) -> p kc s nn", p=128, nn=1024)[:, :, s_, g * 128:(g + 1) * 128])
                for s in range(3):
                    cc = s * 8 + g
                    dep(r=["inbt"], w=["lns"])
                    C_("vector", lambda e, cc=cc: e.tensor_scalar_mul(lns[:, 10:11], inbt[:, cc:cc + 1], -1.0))
                    dep(r=["lns"], w=[r1(B1_PAD[s], B1_PAD[s] + 2)])
                    C_("vector", lambda e, s=s: e.tensor_copy(big1[:, B1_PAD[s]:B1_PAD[s] + 2], lns[:, 10:11].broadcast_to([128, 2])))
                    for tc in range(4):
                        pso = ps_x[:, 0:512] if tc % 2 == 0 else ps_x[:, 512:1024]
                        psr = "ps_x"
                        for kc in range(NKC):
                            dep(r=[r3(0, 3072), ("hT", tc * 512, (tc + 1) * 512)], w=[psr])
                            C_("tensor", lambda e, s=s, tc=tc, kc=kc, g=g, pso=pso: e.matmul(
                                pso, inw3[:, kc, s, :],
                                hT[:, kc, tc * 512:(tc + 1) * 512], start=(kc == 0), stop=(kc == 7)))
                        dep(r=[psr], w=[r1(B1_PAD[s] + 2 + tc * 512, B1_PAD[s] + 2 + (tc + 1) * 512)])
                        C_("scalar", lambda e, s=s, tc=tc, pso=pso: e.activation(
                            big1[:, B1_PAD[s] + 2 + tc * 512:B1_PAD[s] + 2 + (tc + 1) * 512], pso, AFC))

            inproj(0)
            for g in range(NG):
                for s in range(3):
                    cc = s * 8 + g
                    dsts = [B2_X0, B2_X1, B2_VIN][s]
                    rpad = r1(B1_PAD[s], B1_PAD[s] + 2 + L)
                    rdst = r2(dsts, dsts + L)
                    dep(r=[rpad, "swt", "bprime"], w=[rdst])
                    C_("vector", lambda e, s=s, cc=cc, dsts=dsts: e.tensor_scalar(
                        b2(dsts, L), big1[:, B1_PAD[s]:B1_PAD[s] + L],
                        swt[:, cc, 0:1], bprime[:, cc:cc + 1], ALU.mult, ALU.add))
                    dep(r=[rpad, "swt", rdst], w=[rdst])
                    C_("vector", lambda e, s=s, cc=cc, dsts=dsts: e.scalar_tensor_tensor(
                        b2(dsts, L), big1[:, B1_PAD[s] + 1:B1_PAD[s] + 1 + L],
                        swt[:, cc, 1:2], b2(dsts, L), ALU.mult, ALU.add))
                    dep(r=[rpad, "swt", rdst], w=[rdst])
                    C_("vector", lambda e, s=s, cc=cc, dsts=dsts: e.scalar_tensor_tensor(
                        b2(dsts, L), big1[:, B1_PAD[s] + 2:B1_PAD[s] + 2 + L],
                        swt[:, cc, 2:3], b2(dsts, L), ALU.mult, ALU.add))
                dep(r=[r2(B2_VIN, B2_VIN + L), r2(B2_X1, B2_X1 + L)], w=[r2(B2_VIN, B2_VIN + L)])
                C_("vector", lambda e: e.tensor_mul(b2(B2_VIN, L), b2(B2_VIN, L), b2(B2_X1, L)))
                if g == 0:
                    DBG("vin0", b2(B2_VIN, L), (128, L), BF16, r=[r2(B2_VIN, B2_VIN + L)])
                    DBG("x00", b2(B2_X0, L), (128, L), BF16, r=[r2(B2_X0, B2_X0 + L)])
                dep(r=[R_THAT, "acol"], w=[r1(B1_DEC, B1_DEC + L)])
                C_("scalar", lambda e, g=g: e.activation(b1(B1_DEC, L), thatrep, AF.Exp, scale=acol[:, g:g + 1]))
                for qh in range(4):
                    pso, psr = (ps_a, "ps_a") if qh % 2 == 0 else (ps_b, "ps_b")
                    dep(r=["w3b", "h3"], w=[psr])
                    C_("tensor", lambda e, g=g, qh=qh, pso=pso: e.matmul(
                        pso[:, 0:512], w3b[0:64, g * 128:(g + 1) * 128], h3[0:64, qh * 512:(qh + 1) * 512], start=True, stop=True))
                    dep(r=[psr, r1(B1_DEC + qh * 512, B1_DEC + (qh + 1) * 512)], w=[r2(B2_X1 + qh * 512, B2_X1 + (qh + 1) * 512)])
                    C_("vector", lambda e, qh=qh, pso=pso: e.tensor_mul(big2[:, B2_X1 + qh * 512:B2_X1 + (qh + 1) * 512], pso[:, 0:512], big1[:, B1_DEC + qh * 512:B1_DEC + (qh + 1) * 512]))
                if g == 0:
                    DBG("kdense0", b2(B2_X1, L), (128, L), BF16, r=[r2(B2_X1, B2_X1 + L)])
                fwd_fft(B2_X1, B2_K2)
                fwd_fft(B2_VIN, B2_U)
                if g == 0:
                    DBG("U0", b2(B2_U, 4224), (128, 4224), BF16, r=[r2(B2_U, B2_U + 4224)])
                    DBG("K0", b2(B2_K2, 4224), (128, 4224), BF16, r=[r2(B2_K2, B2_K2 + 4224)])
                def vw(off, r):
                    return b2(off, 4224).rearrange("p (a r n) -> p a r n", a=33, r=2)[:, :, r, :]
                RU = r2(B2_U, B2_U + 4224)
                RK = r2(B2_K2, B2_K2 + 4224)
                sp1 = b2(B2_SB, 2112).rearrange("p (a n) -> p a n", a=33)
                sp2 = b2(B2_SB + 2112, 2112).rearrange("p (a n) -> p a n", a=33)
                dep(r=[RU, RK], w=[r2(B2_SB, B2_SB + 2112)])
                C_("vector", lambda e: e.tensor_tensor(sp1, vw(B2_U, 0), vw(B2_K2, 1), ALU.mult))
                dep(r=[RU, RK], w=[r2(B2_SB + 2112, B2_SB + 4224)])
                C_("gpsimd", lambda e: e.tensor_tensor(sp2, vw(B2_U, 1), vw(B2_K2, 1), ALU.mult))
                dep(r=[RU, RK], w=[RU])
                C_("vector", lambda e: e.tensor_tensor(vw(B2_U, 1), vw(B2_U, 1), vw(B2_K2, 0), ALU.mult))
                dep(r=[RU, RK], w=[RU])
                C_("gpsimd", lambda e: e.tensor_tensor(vw(B2_U, 0), vw(B2_U, 0), vw(B2_K2, 0), ALU.mult))
                dep(r=[RU, r2(B2_SB + 2112, B2_SB + 4224)], w=[RU])
                C_("gpsimd", lambda e: e.tensor_tensor(vw(B2_U, 0), vw(B2_U, 0), sp2, ALU.subtract))
                dep(r=[RU, r2(B2_SB, B2_SB + 2112)], w=[RU])
                C_("vector", lambda e: e.tensor_tensor(vw(B2_U, 1), vw(B2_U, 1), sp1, ALU.add))
                if g == 0:
                    DBG("Ymul", b2(B2_U, 4224), (128, 4224), BF16, r=[RU])
                if g + 1 < NG:
                    inproj(g + 1)
                dep(r=["w2b16d"], w=[r1(B1_DEC, B1_DEC + 2048)])
                D_("sync", big1[:, B1_DEC:B1_DEC + 2048], w2b16_d)
                if g == 0:
                    DBG("w2g", big1[:, B1_DEC:B1_DEC + 2048], (128, 2048), BF16, r=[r1(B1_DEC, B1_DEC + 2048)])
                inv_fft(B2_U, g)
                if g == 0:
                    DBG("y0", ycube[:, 0:L], (128, L), BF16, r=[("ycube", 0, L)])
                drain_wq(13)

        if phase_limit >= 5:
            load_reps([(2, 0)])
            for hf in range(2):
                dep(w=[r3(0, 4096)])
                D_("gpsimd", big3[:, 0:4096].rearrange("p (kc n) -> p kc n", n=1024),
                   IN["out_w"].rearrange("(kc p) n -> p kc n", p=128)[:, hf * 4:(hf + 1) * 4, :])
                for kc4 in range(4):
                    kc = hf * 4 + kc4
                    dep(r=[r3(kc4 * 1024, (kc4 + 1) * 1024), ("reps", 0, 1)], w=[r1(kc * 1024, (kc + 1) * 1024)])
                    C_("vector", lambda e, kc=kc, kc4=kc4: e.tensor_mul(big1[:, kc * 1024:(kc + 1) * 1024], big3[:, kc4 * 1024:(kc4 + 1) * 1024], reps[:, 0, :]))
            dep(w=["qa"])
            D_("sync", qa[0:1, 0:1024], IN["out_b"])
            dep(r=["qa"], w=["qcb"])
            C_("vector", lambda e: e.tensor_copy(qcb[0:1, 0:1024], qa[0:1, 0:1024]))
            for hh in range(2):
                dep(r=["qcb", "onesrow"], w=["ps_w"])
                C_("tensor", lambda e, hh=hh: e.matmul(ps_w[:, hh * 512:(hh + 1) * 512], onesrow[:], qcb[0:1, hh * 512:(hh + 1) * 512], start=True, stop=True))
            dep(r=["ps_w", ("reps", 0, 1)], w=[r1(8192, 9216)])
            C_("vector", lambda e: e.tensor_mul(b1(8192, 1024), ps_w[:].rearrange("p n -> p n"), reps[:, 0, :]))
            for tt in range(NT):
                psp, pspr = (ps_w, "ps_w") if tt % 2 == 0 else (ps_x, "ps_x")
                for kc in range(NKC):
                    for hh in range(2):
                        dep(r=["ycube", r1(0, 8192)], w=[pspr])
                        C_("tensor", lambda e, tt=tt, kc=kc, hh=hh, psp=psp: e.matmul(
                            psp[:, hh * 512:(hh + 1) * 512], ycube[:, kc * L + tt * 128:kc * L + (tt + 1) * 128],
                            big1[:, kc * 1024 + hh * 512:kc * 1024 + (hh + 1) * 512],
                            start=(kc == 0), stop=(kc == 7)))
                dep(r=[pspr, ("xres", tt, tt + 1)], w=[("xres", tt, tt + 1)])
                C_("vector", lambda e, tt=tt, psp=psp: e.tensor_add(xres[:, tt, :], xres[:, tt, :], psp[:]))
                dep(r=[r1(8192, 9216), ("xres", tt, tt + 1)], w=[("xres", tt, tt + 1)])
                C_("vector", lambda e, tt=tt: e.tensor_add(xres[:, tt, :], xres[:, tt, :], b1(8192, 1024)))
            DBG("xattn", xres[:].rearrange("p T d -> p (T d)"), (128, NT * D), BF16, r=["xres"])

        drain_wq(99)
        if phase_limit >= 6:
            load_reps([(4, 0), (3, 1)])
            for tt in range(NT):
                xt = xres[:, tt, :]
                rx = ("xres", tt, tt + 1)
                rht = ("hT", tt * 128, (tt + 1) * 128)
                ln_stats(xt, tt)
                dep(r=[rx, "lns"], w=["qa"])
                C_("scalar", lambda e, xt=xt: e.activation(qa[:, 0:D], xt, AF.Identity, bias=lns[:, 8:9], scale=lns[:, 7:8]))
                dep(r=["qa", ("reps", 0, 1)], w=["qa"])
                C_("vector", lambda e: e.tensor_mul(qa[:, 0:D], qa[:, 0:D], reps[:, 0, :]))
                dep(r=["qa", ("reps", 1, 2)], w=["qa"])
                C_("vector", lambda e: e.tensor_add(qa[:, 0:D], qa[:, 0:D], reps[:, 1, :]))
                quant_block(qa[:, 0:512], qcb[:, 0:512], 512, big1f[:, 0:512], ("qa", 0, 512), ("qcb", 0, 512), r1(0, 1024), half=0)
                quant_block(qa[:, 512:1024], qcb[:, 512:1024], 512, big1f[:, 512:1024], ("qa", 512, 1024), ("qcb", 512, 1024), r1(1024, 2048), half=1)
                for kc in range(NKC):
                    dep(r=["qcb", "idenb"], w=["ps_t2"])
                    C_("tensor", lambda e, kc=kc: e.transpose(ps_t2[:, kc * 128:(kc + 1) * 128], qcb[:, kc * 128:(kc + 1) * 128], idenb[:]))
                dep(r=["ps_t2"], w=[rht])
                C_("scalar", lambda e, tt=tt: e.activation(hT[:, :, tt * 128:(tt + 1) * 128], ps_t2[:].rearrange("p (kc n) -> p kc n", n=128), AFC))
            DBG("h2qT", hT[:].rearrange("p kc t -> p (kc t)"), (128, NKC * L), BF16, r=["hT"])

        if phase_limit >= 7:
            def hqT_ap(tth, fc):
                idx = tth * NFC + fc
                if idx < 128:
                    return ycube[:, idx * 128:(idx + 1) * 128], ("ycube", idx * 128, (idx + 1) * 128)
                idx -= 128
                return big1[:, idx * 128:(idx + 1) * 128], ("big1", idx * 128, (idx + 1) * 128)

            load_reps([(5, 0)])
            dep(r=["wdqT"], w=["big2"])
            D_("sync", big2[:, 0:NFC * 1024].rearrange("p (fc n) -> p fc n", n=1024),
               wdqT_d.rearrange("(fc p) n -> p fc n", p=128))
            wgs = big3[:, 0:4096]
            wus = big1[:, 8192:12288]

            def dproj(half, tth):
                tt = half * 8 + tth
                ho2 = (tth % 2) * 512
                psp, pspr = (ps_w, "ps_w") if tth % 2 == 0 else (ps_x, "ps_x")
                for fc in range(NFC):
                    lh, lhr = hqT_ap(tth, fc)
                    for hh in range(2):
                        dep(r=[lhr, ("big2", fc * 1024, (fc + 1) * 1024)], w=[pspr])
                        C_("tensor", lambda e, fc=fc, hh=hh, lh=lh, psp=psp: e.matmul(
                            psp[:, hh * 512:(hh + 1) * 512], lh,
                            big2[:, fc * 1024 + hh * 512:fc * 1024 + (hh + 1) * 512],
                            start=(fc == 0), stop=(fc == NFC - 1)))
                for hh in range(2):
                    dep(r=[pspr, ("reps", 0, 1)], w=[("qa", ho2, ho2 + 512)])
                    C_("vector", lambda e, hh=hh, ho2=ho2, psp=psp: e.tensor_mul(
                        qa[:, ho2:ho2 + 512], psp[:, hh * 512:(hh + 1) * 512], reps[:, 0, hh * 512:(hh + 1) * 512]))
                    dep(r=[("qa", ho2, ho2 + 512), ("xres", tt, tt + 1)], w=[("qa", ho2, ho2 + 512)])
                    C_("vector", lambda e, hh=hh, ho2=ho2, tt=tt: e.tensor_add(
                        qa[:, ho2:ho2 + 512], qa[:, ho2:ho2 + 512], xres[:, tt, hh * 512:(hh + 1) * 512]))
                    dep(r=[("qa", ho2, ho2 + 512)], w=[("outd", tt, tt + 1)])
                    D_("sync", out_d.rearrange("(T p) d -> p T d", p=128)[:, tt, hh * 512:(hh + 1) * 512], qa[:, ho2:ho2 + 512])
            CHUNKS = [(i * 512, 512) for i in range(5)] + [(2560, 256)]
            chain_i = 0
            for half in range(2):
                for (n0, cw) in CHUNKS:
                    dep(r=["wgqT"], w=[r3(0, 4096)])
                    D_("sync", wgs[:, 0:NKC * cw].rearrange("p (kc n) -> p kc n", n=cw),
                       wgqT_d.rearrange("(kc p) n -> p kc n", p=128)[:, :, n0:n0 + cw])
                    dep(r=["wuqT"], w=[r1(8192, 12288)])
                    D_("gpsimd", wus[:, 0:NKC * cw].rearrange("p (kc n) -> p kc n", n=cw),
                       wuqT_d.rearrange("(kc p) n -> p kc n", p=128)[:, :, n0:n0 + cw])
                    for tth in range(8):
                        tt = half * 8 + tth
                        hq = chain_i % 2
                        chain_i += 1
                        ho = hq * 512
                        psg, psgr = (ps_a, "ps_a") if hq == 0 else (ps_b, "ps_b")
                        psu, psur = (ps_w, "ps_w") if hq == 0 else (ps_x, "ps_x")
                        pst, pstr = (ps_t2, "ps_t2") if hq == 0 else (ps_t1, "ps_t1")
                        rht = ("hT", tt * 128, (tt + 1) * 128)
                        for kc in range(NKC):
                            dep(r=[rht, r3(0, 4096)], w=[psgr])
                            C_("tensor", lambda e, tt=tt, kc=kc, cw=cw, psg=psg: e.matmul(
                                psg[:, 0:cw], hT[:, kc, tt * 128:(tt + 1) * 128],
                                wgs[:, kc * cw:(kc + 1) * cw], start=(kc == 0), stop=(kc == 7)))
                        for kc in range(NKC):
                            dep(r=[rht, r1(8192, 12288)], w=[psur])
                            C_("tensor", lambda e, tt=tt, kc=kc, cw=cw, psu=psu: e.matmul(
                                psu[:, 0:cw], hT[:, kc, tt * 128:(tt + 1) * 128],
                                wus[:, kc * cw:(kc + 1) * cw], start=(kc == 0), stop=(kc == 7)))
                        dep(r=[psgr], w=[("qcc", ho, ho + cw)])
                        C_("scalar", lambda e, psg=psg, ho=ho, cw=cw: e.activation(qcc[:, ho:ho + cw], psg[:, 0:cw], AF.Silu))
                        dep(r=[("qcc", ho, ho + cw), psur], w=[("qa", ho, ho + cw)])
                        C_("vector", lambda e, psu=psu, ho=ho, cw=cw: e.tensor_tensor(qa[:, ho:ho + cw], qcc[:, ho:ho + cw], psu[:, 0:cw], ALU.mult))
                        if half == 0 and n0 == 0 and tth == 0:
                            DBG("hmid0", qa[:, 0:256], (128, 256), r=["qa"])
                        quant_block(qa[:, ho:ho + cw], qcb[:, ho:ho + cw], cw, big1f[:, 3072 + hq * 512:3584 + hq * 512], ("qa", ho, ho + cw), ("qcb", ho, ho + cw), r1(6144 + hq * 1024, 7168 + hq * 1024), half=hq)
                        for j in range(cw // 128):
                            dep(r=[("qcb", ho, ho + cw), "idenb"], w=[pstr])
                            C_("tensor", lambda e, j=j, ho=ho, pst=pst: e.transpose(pst[:, j * 128:(j + 1) * 128], qcb[:, ho + j * 128:ho + (j + 1) * 128], idenb[:]))
                        for j in range(cw // 128):
                            fc = n0 // 128 + j
                            dst, dstr = hqT_ap(tth, fc)
                            dep(r=[pstr], w=[dstr])
                            C_("vector", lambda e, j=j, dst=dst, pst=pst: e.tensor_copy(dst, pst[:, j * 128:(j + 1) * 128]))
                for tth in range(8):
                    dproj(half, tth)


        OWN = dict(ENG_SEM)

        def emit(e, name):
            clock = dict.fromkeys(SEMS, 0)
            own = OWN.get(name)
            for (eng, fn, sem, val, deps, inc) in PLAN:
                if eng != name:
                    continue
                for s, v in deps.items():
                    if s == own and own == "ct":
                        continue
                    if v > clock[s]:
                        e.wait_ge(SEMOBJ[s], v)
                        clock[s] = v
                fn(e).then_inc(SEMOBJ[sem], inc)
                if sem == own and own == "ct":
                    clock[sem] = val

        @block.sync
        def _(e):
            emit(e, "sync")
            for s in SEMS:
                if cum[s]:
                    e.wait_ge(SEMOBJ[s], cum[s])

        @block.vector
        def _(e):
            emit(e, "vector")

        @block.scalar
        def _(e):
            emit(e, "scalar")

        @block.gpsimd
        def _(e):
            emit(e, "gpsimd")

        @block.tensor
        def _(e):
            emit(e, "tensor")

    _ctx.close()
    return nc


def _host_inputs(inputs, b):
    cc = _build_consts()
    sw = np.asarray(inputs["L0_attn_short_w"], np.float32).reshape(24, 128, 3).transpose(1, 0, 2).reshape(128, 72)
    inb = np.asarray(inputs["L0_attn_in_b"], np.float32).reshape(24, 128).T
    sb = np.asarray(inputs["L0_attn_short_b"], np.float32).reshape(24, 128).T
    fb = np.asarray(inputs["L0_attn_filter_bias"], np.float32).reshape(8, 128).T
    m = {
        "u": np.ascontiguousarray(np.asarray(inputs["L0_attn_u"], np.float32)[b]),
        "c_row": np.ascontiguousarray(np.asarray(inputs["L0_c"], np.float32)[b:b + 1]),
        "in_w": np.asarray(inputs["L0_attn_in_w"], np.float32),
        "swp": np.ascontiguousarray(sw), "inbp": np.ascontiguousarray(inb),
        "sbp": np.ascontiguousarray(sb), "fbp": np.ascontiguousarray(fb),
        "mlp_w0": np.asarray(inputs["L0_attn_mlp_w0"], np.float32),
        "mlp_b0": np.ascontiguousarray(np.asarray(inputs["L0_attn_mlp_b0"], np.float32).reshape(-1)[:, None]),
        "freq0": np.ascontiguousarray(np.asarray(inputs["L0_attn_freq0"], np.float32).reshape(-1)[:, None]),
        "mlp_w1": np.asarray(inputs["L0_attn_mlp_w1"], np.float32),
        "mlp_b1": np.ascontiguousarray(np.asarray(inputs["L0_attn_mlp_b1"], np.float32).reshape(-1)[:, None]),
        "freq1": np.ascontiguousarray(np.asarray(inputs["L0_attn_freq1"], np.float32).reshape(-1)[:, None]),
        "mlp_w2": np.asarray(inputs["L0_attn_mlp_w2"], np.float32),
        "mlp_b2": np.ascontiguousarray(np.asarray(inputs["L0_attn_mlp_b2"], np.float32).reshape(-1)[:, None]),
        "freq2": np.ascontiguousarray(np.asarray(inputs["L0_attn_freq2"], np.float32).reshape(-1)[:, None]),
        "mlp_w3": np.asarray(inputs["L0_attn_mlp_w3"], np.float32),
        "out_w": np.asarray(inputs["L0_attn_out_w"], np.float32),
        "out_b": np.asarray(inputs["L0_attn_out_b"], np.float32)[None, :],
        "w_gate": np.asarray(inputs["L0_ffn_w_gate"], np.float32),
        "w_up": np.asarray(inputs["L0_ffn_w_up"], np.float32),
        "w_down": np.asarray(inputs["L0_ffn_w_down"], np.float32),
        "wmod": np.asarray(inputs["L0_wmod"], np.float32),
    }
    m.update(cc)
    return m


def kernel(**inputs):
    nc = build_program()
    in_maps = [_host_inputs(inputs, b) for b in range(8)]
    res = run_bass_kernel_spmd(nc, in_maps, core_ids=list(range(8)))
    out = np.stack([np.asarray(res.results[b]["out"], np.float32) for b in range(8)], axis=0)
    return out


# version 13 (final)
# speedup vs baseline: 1.0170x; 1.0117x over previous
import numpy as np
import ml_dtypes
import concourse.bass as bass
import concourse.mybir as mybir
from concourse.bass_utils import run_bass_kernel_spmd

F32 = mybir.dt.float32
BF16 = mybir.dt.bfloat16
I32 = mybir.dt.int32
AF = mybir.ActivationFunctionType
ALU = mybir.AluOpType
AX = mybir.AxisListType
bfloat16 = ml_dtypes.bfloat16

L = 2048
D = 1024
TW = 3072
FI = 2816
NT = 16
NG = 8
NKC = 8
NFC = 22

_CONSTS = None

def _build_consts():
    global _CONSTS
    if _CONSTS is not None:
        return _CONSTS
    S1C = np.zeros((128, 264))
    for q in range(4):
        for n1 in range(32):
            p = q * 32 + n1
            for a in range(33):
                ang = 2 * np.pi * n1 * a / 64.0
                S1C[p, q * 66 + a * 2 + 0] = np.cos(ang)
                S1C[p, q * 66 + a * 2 + 1] = -np.sin(ang)
    UH = np.zeros((33, 128, 128))
    for a in range(33):
        for n2 in range(64):
            Tr = np.cos(2 * np.pi * n2 * a / 4096.0)
            Ti = -np.sin(2 * np.pi * n2 * a / 4096.0)
            for g in range(64):
                wr = np.cos(2 * np.pi * n2 * g / 64.0)
                wi = -np.sin(2 * np.pi * n2 * g / 64.0)
                UH[a, 0 * 64 + n2, 0 * 64 + g] = Tr * wr - Ti * wi
                UH[a, 1 * 64 + n2, 0 * 64 + g] = -(Tr * wi + Ti * wr)
                UH[a, 0 * 64 + n2, 1 * 64 + g] = Tr * wi + Ti * wr
                UH[a, 1 * 64 + n2, 1 * 64 + g] = Tr * wr - Ti * wi
    UHs = np.concatenate([UH[a] for a in range(33)], axis=1)
    W1 = np.zeros((128, 128))
    for g in range(64):
        for n2o in range(64):
            c = np.cos(2 * np.pi * n2o * g / 64.0)
            s = np.sin(2 * np.pi * n2o * g / 64.0)
            W1[0 * 64 + g, n2o * 2 + 0] = c
            W1[1 * 64 + g, n2o * 2 + 0] = -s
            W1[0 * 64 + g, n2o * 2 + 1] = s
            W1[1 * 64 + g, n2o * 2 + 1] = c
    W2 = np.zeros((128, 64 * 32))
    for n2o in range(64):
        for a in range(33):
            rho = 1.0 if (a == 0 or a == 32) else 2.0
            for m in range(32):
                t = 64 * m + n2o
                ang = 2 * np.pi * t * a / 4096.0
                W2[a * 2 + 0, n2o * 32 + m] = rho * np.cos(ang) / 4096.0
                W2[a * 2 + 1, n2o * 32 + m] = -rho * np.sin(ang) / 4096.0
    t_hat = np.linspace(0.0, 1.0, L)
    w = 2.0 * np.pi * np.linspace(0.0, L - 1.0, L) / L
    phase = 1e-4 * w
    zT = np.stack([t_hat, np.cos(phase), -np.sin(phase)], axis=0)
    MAX_DECAY = np.log(0.01) / 0.3
    MIN_DECAY = np.log(0.01) / 1.5
    deltas = np.abs(np.linspace(MIN_DECAY, MAX_DECAY, 1024))
    _CONSTS = {
        "cc_s1c": S1C.astype(np.float32),
        "cc_uhat": UHs.astype(np.float32),
        "cc_w1": W1.astype(np.float32),
        "cc_w2": W2.astype(np.float32),
        "cc_zt": np.ascontiguousarray(zT.astype(np.float32)),
        "cc_adelta": np.ascontiguousarray(deltas.reshape(8, 128).T.astype(np.float32)),
        "cc_that": t_hat[None, :].astype(np.float32),
        "cc_iden": np.eye(128, dtype=np.float32),
    }
    return _CONSTS

INPUT_SHAPES = {
    "u": (L, D), "c_row": (1, D),
    "in_w": (D, TW), "swp": (128, 72), "inbp": (128, 24), "sbp": (128, 24),
    "mlp_w0": (3, 64), "mlp_b0": (64, 1), "freq0": (64, 1),
    "mlp_w1": (64, 64), "mlp_b1": (64, 1), "freq1": (64, 1),
    "mlp_w2": (64, 64), "mlp_b2": (64, 1), "freq2": (64, 1),
    "mlp_w3": (64, D), "fbp": (128, 8),
    "out_w": (D, D), "out_b": (1, D),
    "w_gate": (FI, D), "w_up": (FI, D), "w_down": (D, FI),
    "wmod": (D, 6 * D),
    "cc_s1c": (128, 264), "cc_uhat": (128, 33 * 128), "cc_w1": (128, 128),
    "cc_w2": (128, 64 * 32), "cc_zt": (3, L), "cc_adelta": (128, 8),
    "cc_that": (1, L), "cc_iden": (128, 128),
}

BIG = 1 << 60


def build_program(dbg=(), phase_limit=99):
    nc = bass.Bass("TRN2", target_bir_lowering=False)
    IN = {}
    for name, shp in INPUT_SHAPES.items():
        IN[name] = nc.dram_tensor(name, list(shp), F32, kind="ExternalInput").ap()
    out_d = nc.dram_tensor("out", [L, D], F32, kind="ExternalOutput").ap()
    wgqT_d = nc.dram_tensor("wgqT", [D, FI], BF16, kind="Internal").ap()
    wuqT_d = nc.dram_tensor("wuqT", [D, FI], BF16, kind="Internal").ap()
    wdqT_d = nc.dram_tensor("wdqT", [FI, D], BF16, kind="Internal").ap()
    modrow_d = nc.dram_tensor("modrow_d", [1, 6 * D], BF16, kind="Internal").ap()
    w2b16_d = nc.dram_tensor("w2b16_d", [128, 2048], BF16, kind="Internal").ap()

    SEMS = ["cv", "cs", "ct", "cp", "dS0", "dS1", "dS2", "dG0", "dG1", "dG2"]
    ENG_SEM = {"vector": "cv", "scalar": "cs", "tensor": "ct", "gpsimd": "cp"}
    QUEUE_ROT = {"sync": ["dS0", "dS1", "dS2"], "gpsimd": ["dG0", "dG1", "dG2"]}
    QCNT = {"sync": 0, "gpsimd": 0}
    cum = dict.fromkeys(SEMS, 0)
    resW = {}
    resR = {}
    floor = {}
    PEND = {"on": False, "r": [], "w": []}
    PLAN = []

    def _ivl(spec):
        if isinstance(spec, str):
            return (spec, 0, BIG)
        return spec

    def dep(r=(), w=()):
        PEND["r"] = [_ivl(x) for x in r]
        PEND["w"] = [_ivl(x) for x in w]
        PEND["on"] = True

    def st(eng, fn, kind):
        if kind == "dma":
            rot = QUEUE_ROT[eng]
            sem = rot[QCNT[eng] % len(rot)]
            QCNT[eng] += 1
        else:
            sem = ENG_SEM[eng]
        inc = 16 if kind == "dma" else 1
        deps = dict(floor)
        if kind == "dma" and cum[sem] > deps.get(sem, 0):
            deps[sem] = cum[sem]
        if PEND["on"]:
            rs, ws = PEND["r"], PEND["w"]
            PEND["on"] = False
            annotated = True
            for (n, lo, hi) in rs:
                for (l2, h2, s2, v2) in resW.get(n, ()):
                    if l2 < hi and lo < h2:
                        if deps.get(s2, 0) < v2:
                            deps[s2] = v2
            for (n, lo, hi) in ws:
                for lst in (resW, resR):
                    for (l2, h2, s2, v2) in lst.get(n, ()):
                        if l2 < hi and lo < h2:
                            if deps.get(s2, 0) < v2:
                                deps[s2] = v2
        else:
            annotated = False
            for s2 in SEMS:
                if cum[s2] > deps.get(s2, 0):
                    deps[s2] = cum[s2]
        val = cum[sem] + inc
        cum[sem] = val
        if annotated:
            for (n, lo, hi) in rs:
                resR.setdefault(n, []).append((lo, hi, sem, val))
            for (n, lo, hi) in ws:
                for lst in (resW, resR):
                    if n in lst:
                        lst[n] = [e for e in lst[n] if not (lo <= e[0] and e[1] <= hi)]
                resW.setdefault(n, []).append((lo, hi, sem, val))
        else:
            floor.clear()
            floor[sem] = val
            for k in list(resW):
                resW[k] = []
            for k in list(resR):
                resR[k] = []
        PLAN.append((eng, fn, sem, val, deps, inc))

    def D_(eng, out_ap, in_ap, **kw):
        def f(e, o=out_ap, i=in_ap, kw=kw):
            return e.dma_start(out=o, in_=i, **kw)
        st(eng, f, "dma")

    def C_(eng, fn):
        st(eng, fn, "compute")

    def DBG(name, ap, shape, dtype=F32, r=()):
        if name in dbg:
            d = nc.dram_tensor("dbg_" + name, list(shape), dtype, kind="ExternalOutput").ap()
            if r:
                dep(r=r)
            D_("sync", d, ap)

    from contextlib import ExitStack
    _ctx = ExitStack()
    E = _ctx.enter_context
    xres = E(nc.sbuf_tensor("xres", [128, NT, D], BF16))
    hT = E(nc.sbuf_tensor("hT", [128, NKC, L], BF16))
    ycube = E(nc.sbuf_tensor("ycube", [128, NG * L], BF16))
    big1 = E(nc.sbuf_tensor("big1", [128, 12320], BF16))
    big2 = E(nc.sbuf_tensor("big2", [128, 24960], BF16))
    big3 = E(nc.sbuf_tensor("big3", [128, 4096], BF16))
    qa = E(nc.sbuf_tensor("qa", [128, 1024], F32))
    qca = E(nc.sbuf_tensor("qca", [128, 1024], BF16))
    qcb = E(nc.sbuf_tensor("qcb", [128, 1024], BF16))
    qcc = E(nc.sbuf_tensor("qcc", [128, 1024], BF16))
    s1cb = E(nc.sbuf_tensor("s1cb", [128, 264], BF16))
    uhatb = E(nc.sbuf_tensor("uhatb", [128, 33 * 128], BF16))
    w1b = E(nc.sbuf_tensor("w1b", [128, 128], BF16))
    idenb = E(nc.sbuf_tensor("idenb", [128, 128], BF16))
    w3b = E(nc.sbuf_tensor("w3b", [64, D], BF16))
    h3 = E(nc.sbuf_tensor("h3", [64, L], BF16))
    mlpc = E(nc.sbuf_tensor("mlpc", [64, 8], F32))
    swt = E(nc.sbuf_tensor("swt", [128, 24, 3], F32))
    inbt = E(nc.sbuf_tensor("inbt", [128, 24], F32))
    bprime = E(nc.sbuf_tensor("bprime", [128, 24], F32))
    fbt = E(nc.sbuf_tensor("fbt", [128, 8], F32))
    acol = E(nc.sbuf_tensor("acol", [128, 8], F32))
    reps = E(nc.sbuf_tensor("reps", [128, 2, D], BF16))
    onesrow = E(nc.sbuf_tensor("onesrow", [1, 128], BF16))
    lns = E(nc.sbuf_tensor("lns", [128, 16], F32))
    qm = E(nc.sbuf_tensor("qm", [128, 64], F32))
    qr = E(nc.sbuf_tensor("qr", [128, 64], F32))
    qsb = E(nc.sbuf_tensor("qsb", [128, 64], BF16))
    ps_a = E(nc.psum_tensor("ps_a", [128, 512], F32))
    ps_b = E(nc.psum_tensor("ps_b", [128, 512], F32))
    ps_w = E(nc.psum_tensor("ps_w", [128, 1024], F32))
    ps_x = E(nc.psum_tensor("ps_x", [128, 1024], F32))
    ps_t1 = E(nc.psum_tensor("ps_t1", [128, 1024], BF16))
    ps_t2 = E(nc.psum_tensor("ps_t2", [128, 1024], BF16))
    SEMOBJ = {s: E(nc.semaphore("sem_" + s)) for s in SEMS}
    block = E(nc.Block())
    if True:
        B1_PAD = [0, 2052, 4104]
        B1_V1 = 6156
        B1_DEC = 8204
        B1_LN = 10252
        B1_H = 11276
        B2_X1 = 0
        B2_VIN = 2048
        B2_X0 = 4096
        B2_SA = 6144
        B2_SB = 10368
        B2_K2 = 14592
        B2_U = B2_SA

        def b2(off, ln):
            return big2[:, off:off + ln]

        def b1(off, ln):
            return big1[:, off:off + ln]

        def r1(off, ln):
            return ("big1", off, off + ln)

        def r2(off, ln):
            return ("big2", off, off + ln)

        def r3(off, ln):
            return ("big3", off, off + ln)

        RYW = lambda lo, hi: ("ycube", lo, hi)

        big1f = big1[:].bitcast(F32)
        ycubef = ycube[:].bitcast(F32)
        big2f = big2[:].bitcast(F32)
        mlpw = big2f[0:64, 0:192]
        big3f = big3[:].bitcast(F32)
        thatrep = big2[:, 18816:18816 + 2048]
        R_THAT = r2(18816, 20864)
        modrow = big1[0:1, 0:6144]
        AFC = AF.Copy

        def quant_block(src_ap, dst_ap, W, absf, src_r, dst_r, abs_r, half=None):
            Bn = W // 16
            if half is None:
                qo, bo = 0, 0
            else:
                qo, bo = half * 512, half * 32
            Rq = lambda nm: (nm, bo, bo + Bn)
            Rb = lambda nm: (nm, qo, qo + W)
            qmv = qm[:, bo:bo + Bn]
            qrv = qr[:, bo:bo + Bn]
            qsbv = qsb[:, bo:bo + Bn]
            win = lambda t: t.rearrange("p (nb w) -> p nb w", w=16)
            dep(r=[src_r], w=[Rq("qm")])
            C_("vector", lambda e: e.reduce_max(qmv.unsqueeze(2), win(src_ap), axis=AX.X, apply_absolute_value=True))
            dep(r=[Rq("qm")], w=[Rq("qm")])
            C_("vector", lambda e: e.tensor_scalar_max(qmv, qmv, 6e-12))
            dep(r=[Rq("qm")], w=[Rq("qr")])
            C_("vector", lambda e: e.reciprocal(qrv, qmv))
            dep(r=[Rq("qm")], w=[Rq("qsb")])
            C_("vector", lambda e: e.tensor_scalar(qsbv, qmv, 1.0 / 12.0, None, ALU.mult))
            dep(r=[src_r], w=[abs_r])
            C_("scalar", lambda e: e.activation(absf[:, 0:W], src_ap, AF.Abs))
            abs_res = abs_r
            rbc = qrv.unsqueeze(2).broadcast_to([128, Bn, 16])
            dep(r=[Rq("qr"), abs_res], w=[abs_res])
            C_("vector", lambda e: e.scalar_tensor_tensor(win(absf[:, 0:W]), win(absf[:, 0:W]), 6.0, rbc, ALU.mult, ALU.mult))
            u = absf[:, 0:W]
            A, Bf, Cf = qca[:, qo:qo + W], qcb[:, qo:qo + W], qcc[:, qo:qo + W]
            dep(r=[abs_res], w=[Rb("qca")])
            C_("vector", lambda e: e.tensor_single_scalar(A, u, 0.25, ALU.is_ge))
            for th in (0.75, 1.25, 1.75):
                dep(r=[abs_res, Rb("qca")], w=[Rb("qca")])
                C_("vector", lambda e, th=th: e.scalar_tensor_tensor(A, u, th, A, ALU.is_ge, ALU.add))
            dep(r=[abs_res], w=[Rb("qcb")])
            C_("vector", lambda e: e.tensor_single_scalar(Bf, u, 5.0, ALU.is_ge))
            for th in (5.0, 3.5, 2.5):
                dep(r=[abs_res, Rb("qcb")], w=[Rb("qcb")])
                C_("vector", lambda e, th=th: e.scalar_tensor_tensor(Bf, u, th, Bf, ALU.is_ge, ALU.add))
            dep(r=[Rb("qca"), Rb("qcb")], w=[Rb("qca")])
            C_("vector", lambda e: e.scalar_tensor_tensor(A, Bf, 2.0, A, ALU.mult, ALU.add))
            dep(r=[src_r], w=[Rb("qcb")])
            C_("scalar", lambda e: e.activation(Bf, src_ap, AF.Sign))
            sbc = qsbv.unsqueeze(2).broadcast_to([128, Bn, 16])
            dep(r=[Rb("qcb"), Rq("qsb")], w=[Rb("qcc")])
            C_("gpsimd", lambda e: e.tensor_tensor(win(Cf), win(Bf), sbc, ALU.mult))
            dep(r=[Rb("qca"), Rb("qcc")], w=[dst_r])
            C_("gpsimd", lambda e: e.tensor_mul(dst_ap, A, Cf))

        def load_reps(slots):
            for (j, ri) in slots:
                dep(r=["modrowd"], w=["qcb"])
                D_("sync", qcb[0:1, 0:D], modrow_d[0:1, j * D:(j + 1) * D])
                for hh in range(2):
                    dep(r=["qcb", "onesrow"], w=["ps_w"])
                    C_("tensor", lambda e, hh=hh: e.matmul(ps_w[:, hh * 512:(hh + 1) * 512], onesrow[:], qcb[0:1, hh * 512:(hh + 1) * 512], start=True, stop=True))
                dep(r=["ps_w"], w=[("reps", ri, ri + 1)])
                C_("scalar", lambda e, ri=ri: e.activation(reps[:, ri, :], ps_w[:], AFC))

        dep(w=["xres"])
        D_("gpsimd", xres[:], IN["u"].rearrange("(T p) d -> p T d", p=128))
        dep(w=[r3(0, 256)])
        D_("sync", big3f[:, 0:128], IN["cc_iden"])
        dep(r=[r3(0, 256)], w=["idenb"])
        C_("vector", lambda e: e.tensor_copy(idenb[:], big3f[:, 0:128]))
        for nm, t, rn in (("cc_s1c", s1cb, "s1cb"), ("cc_uhat", uhatb, "uhatb"), ("cc_w1", w1b, "w1b")):
            dep(w=[rn])
            D_("gpsimd", t[:], IN[nm])
        dep(w=[r2(0, 4096)])
        D_("sync", big2f[:, 0:2048], IN["cc_w2"])
        dep(r=[r2(0, 4096)], w=[r2(8192, 10240)])
        C_("vector", lambda e: e.tensor_copy(big2[:, 8192:10240], big2f[:, 0:2048]))
        dep(r=[r2(8192, 10240)], w=["w2b16d"])
        D_("sync", w2b16_d, big2[:, 8192:10240])
        dep(w=["w3b"])
        D_("gpsimd", w3b[:], IN["mlp_w3"])
        dep(w=["swt"])
        D_("sync", swt[:], IN["swp"].rearrange("p (cc k) -> p cc k", k=3))
        dep(w=["inbt"])
        D_("sync", inbt[:], IN["inbp"])
        dep(w=["fbt"])
        D_("sync", fbt[:], IN["fbp"])
        dep(w=["acol"])
        D_("sync", acol[:], IN["cc_adelta"])
        dep(r=["acol"], w=["acol"])
        C_("vector", lambda e: e.tensor_scalar_mul(acol[:], acol[:], -1.0))
        dep(w=["onesrow"])
        C_("vector", lambda e: e.memset(onesrow[:], 1.0))
        dep(w=["bprime"])
        D_("sync", bprime[:], IN["sbp"])
        dep(r=["swt"], w=["qm"])
        C_("vector", lambda e: e.reduce_sum(qm[:, 0:24].unsqueeze(2), swt[:], axis=AX.X))
        dep(r=["inbt", "qm"], w=["qr"])
        C_("vector", lambda e: e.tensor_mul(qr[:, 0:24], inbt[:], qm[:, 0:24]))
        dep(r=["qr", "bprime"], w=["bprime"])
        C_("vector", lambda e: e.tensor_add(bprime[:], bprime[:], qr[:, 0:24]))
        dep(w=[R_THAT])
        D_("gpsimd", thatrep, IN["cc_that"].partition_broadcast(128))

        if phase_limit >= 2:
            scf = big1f[0:1, 3072:4096]
            dep(w=[r1(6144, 8192)])
            D_("sync", scf, IN["c_row"])
            dep(r=[r1(6144, 8192)], w=[r1(8192, 10240)])
            C_("scalar", lambda e: e.activation(big1f[0:1, 4096:5120], scf, AF.Silu))
            dep(r=[r1(8192, 10240)], w=["qcb"])
            C_("vector", lambda e: e.tensor_copy(qcb[0:1, 0:D], big1f[0:1, 4096:5120]))
            for j in range(8):
                dep(r=["qcb", "idenb"], w=["ps_t1"])
                C_("tensor", lambda e, j=j: e.transpose(ps_t1[:, 0:128], qcb[0:1, j * 128:(j + 1) * 128], idenb[0:1, :]))
                dep(r=["ps_t1"], w=["qsb"])
                C_("vector", lambda e, j=j: e.tensor_copy(qsb[:, j:j + 1], ps_t1[:, 0:1]))
            for csk in range(12):
                dep(w=[r3(0, 4096)])
                D_("gpsimd", big3[:, 0:NKC * 512].rearrange("p (kc n) -> p kc n", n=512),
                   IN["wmod"].rearrange("(kc p) n -> p kc n", p=128)[:, :, csk * 512:(csk + 1) * 512])
                for kc in range(NKC):
                    dep(r=["qsb", r3(0, 4096)], w=["ps_a"])
                    C_("tensor", lambda e, kc=kc: e.matmul(
                        ps_a[0:1, 0:512], qsb[:, kc:kc + 1],
                        big3[:, kc * 512:(kc + 1) * 512], start=(kc == 0), stop=(kc == 7)))
                dep(r=["ps_a"], w=[r1(csk * 512, (csk + 1) * 512)])
                C_("scalar", lambda e, csk=csk: e.activation(modrow[:, csk * 512:(csk + 1) * 512], ps_a[0:1, 0:512], AFC))
            for j in (2, 5):
                g_ap = modrow[:, j * D:(j + 1) * D]
                dep(r=[r1(j * D, (j + 1) * D)], w=[r1(8192, 9216)])
                C_("scalar", lambda e, g=g_ap: e.activation(b1(8192, D)[0:1, :], g, AF.Abs))
                dep(r=[r1(8192, 9216)], w=[r1(8192, 9216)])
                C_("vector", lambda e: e.tensor_scalar_max(b1(8192, D)[0:1, :], b1(8192, D)[0:1, :], 0.25))
                dep(r=[r1(j * D, (j + 1) * D)], w=[r1(9216, 10240)])
                C_("scalar", lambda e, g=g_ap: e.activation(b1(9216, D)[0:1, :], g, AF.Sign))
                dep(r=[r1(8192, 10240)], w=[r1(j * D, (j + 1) * D)])
                C_("vector", lambda e, g=g_ap: e.tensor_mul(g, b1(8192, D)[0:1, :], b1(9216, D)[0:1, :]))
            for j in (1, 4):
                g_ap = modrow[:, j * D:(j + 1) * D]
                dep(r=[r1(j * D, (j + 1) * D)], w=[r1(j * D, (j + 1) * D)])
                C_("vector", lambda e, g=g_ap: e.tensor_scalar_add(g, g, 1.0))
            DBG("modrow", modrow, (1, 6 * D), BF16, r=[r1(0, 6144)])
            dep(r=[r1(0, 6144)], w=["modrowd"])
            D_("sync", modrow_d, modrow)

            for jj, (bn, fn_) in enumerate((("mlp_b0", "freq0"), ("mlp_b1", "freq1"), ("mlp_b2", "freq2"))):
                dep(w=[("mlpc", 6, 7)])
                D_("sync", mlpc[:, 6:7], IN[bn])
                dep(w=[("mlpc", 2 * jj + 1, 2 * jj + 2)])
                D_("sync", mlpc[:, 2 * jj + 1:2 * jj + 2], IN[fn_])
                dep(r=[("mlpc", 6, 7), ("mlpc", 2 * jj + 1, 2 * jj + 2)], w=[("mlpc", 2 * jj, 2 * jj + 1)])
                C_("vector", lambda e, jj=jj: e.tensor_mul(mlpc[:, 2 * jj:2 * jj + 1], mlpc[:, 6:7], mlpc[:, 2 * jj + 1:2 * jj + 2]))
            dep(w=[r2(0, 128)])
            D_("sync", mlpw[:, 0:64], IN["mlp_w1"])
            dep(w=[r2(128, 256)])
            D_("sync", mlpw[:, 64:128], IN["mlp_w2"])
            dep(w=[r2(256, 384)])
            D_("sync", mlpw[0:3, 128:192], IN["mlp_w0"])
            ztb = big3f[0:3, 0:L]
            dep(w=[r3(0, 4096)])
            D_("sync", ztb, IN["cc_zt"])
            fsc = big1f[0:64, 0:2048]
            qus = big1f[0:64, 2048:4096]
            sinf = big1f[0:64, 4096:6144]
            qai = qa[:].bitcast(I32)
            RFSC = r1(0, 4096)
            RQUS = r1(4096, 8192)
            RSINF = r1(8192, 12288)

            def sin_layer(ci, outf, outr):
                dep(r=[RFSC, "mlpc"], w=[RFSC])
                C_("vector", lambda e, ci=ci: e.tensor_scalar(fsc, fsc, mlpc[0:64, 2 * ci + 1:2 * ci + 2], mlpc[0:64, 2 * ci:2 * ci + 1], ALU.mult, ALU.add))
                dep(r=[RFSC], w=[RQUS])
                C_("vector", lambda e: e.tensor_scalar_mul(qus, fsc, float(1.0 / (2 * np.pi))))
                for hh in range(2):
                    dep(r=[RQUS], w=["qa"])
                    C_("vector", lambda e, hh=hh: e.tensor_copy(qai[0:64, 0:1024], qus[:, hh * 1024:(hh + 1) * 1024]))
                    dep(r=["qa"], w=[RQUS])
                    C_("vector", lambda e, hh=hh: e.tensor_copy(qus[:, hh * 1024:(hh + 1) * 1024], qai[0:64, 0:1024]))
                dep(r=[RQUS, RFSC], w=[RQUS])
                C_("vector", lambda e: e.scalar_tensor_tensor(qus, qus, float(-2 * np.pi), fsc, ALU.mult, ALU.add))
                dep(r=[RQUS], w=[outr])
                C_("scalar", lambda e, outf=outf: e.activation(outf, qus, AF.Sin))

            for ck in range(4):
                dep(r=[r2(0, 384), r3(0, 4096)], w=["ps_b"])
                C_("tensor", lambda e, ck=ck: e.matmul(ps_b[0:64, 0:512], mlpw[0:3, 128:192], ztb[:, ck * 512:(ck + 1) * 512], start=True, stop=True))
                dep(r=["ps_b"], w=[RFSC])
                C_("scalar", lambda e, ck=ck: e.activation(fsc[:, ck * 512:(ck + 1) * 512], ps_b[0:64, 0:512], AFC))
            DBG("fl1", fsc, (64, L), r=[RFSC])
            sin_layer(0, sinf, RSINF)
            DBG("fs0", sinf, (64, L), r=[RSINF])
            for li in range(2):
                for ck in range(4):
                    dep(r=[r2(0, 384), RSINF], w=["ps_b"])
                    C_("tensor", lambda e, li=li, ck=ck: e.matmul(ps_b[0:64, 0:512], mlpw[0:64, li * 64:(li + 1) * 64], sinf[:, ck * 512:(ck + 1) * 512], start=True, stop=True))
                    dep(r=["ps_b"], w=[RFSC])
                    C_("scalar", lambda e, ck=ck: e.activation(fsc[:, ck * 512:(ck + 1) * 512], ps_b[0:64, 0:512], AFC))
                sin_layer(li + 1, sinf if li == 0 else h3[0:64, :], RSINF if li == 0 else "h3")
            DBG("h3", h3[0:64, :], (64, L), BF16, r=["h3"])

        def ln_stats(xt, tt):
            rx = ("xres", tt, tt + 1)
            dep(r=[rx], w=[r1(B1_LN, B1_LN + D), "lns"])
            C_("scalar", lambda e, xt=xt: e.activation(b1(B1_LN, D), xt, AF.Square, accum_out=lns[:, 0:1]))
            dep(r=[rx, "lns"], w=["lns"])
            C_("vector", lambda e, xt=xt: e.reduce_sum(lns[:, 1:2], xt, axis=AX.X))
            dep(r=["lns"], w=["lns"])
            C_("vector", lambda e: e.tensor_scalar_mul(lns[:, 2:3], lns[:, 1:2], 1.0 / D))
            dep(r=["lns"], w=["lns"])
            C_("vector", lambda e: e.tensor_scalar_mul(lns[:, 3:4], lns[:, 0:1], 1.0 / D))
            dep(r=["lns"], w=["lns"])
            C_("vector", lambda e: e.tensor_mul(lns[:, 4:5], lns[:, 2:3], lns[:, 2:3]))
            dep(r=["lns"], w=["lns"])
            C_("vector", lambda e: e.tensor_sub(lns[:, 5:6], lns[:, 3:4], lns[:, 4:5]))
            dep(r=["lns"], w=["lns"])
            C_("vector", lambda e: e.tensor_scalar_add(lns[:, 5:6], lns[:, 5:6], 1e-5))
            dep(r=["lns"], w=["lns"])
            C_("scalar", lambda e: e.activation(lns[:, 6:7], lns[:, 5:6], AF.Sqrt))
            dep(r=["lns"], w=["lns"])
            C_("vector", lambda e: e.reciprocal(lns[:, 7:8], lns[:, 6:7]))
            dep(r=["lns"], w=["lns"])
            C_("vector", lambda e: e.tensor_mul(lns[:, 8:9], lns[:, 2:3], lns[:, 7:8]))
            dep(r=["lns"], w=["lns"])
            C_("vector", lambda e: e.tensor_scalar_mul(lns[:, 8:9], lns[:, 8:9], -1.0))

        if phase_limit >= 3:
            load_reps([(1, 0), (0, 1)])
            for tt in range(NT):
                xt = xres[:, tt, :]
                rx = ("xres", tt, tt + 1)
                rht = ("hT", tt * 128, (tt + 1) * 128)
                ln_stats(xt, tt)
                dep(r=[rx, "lns"], w=[r1(B1_H, B1_H + D)])
                C_("scalar", lambda e, xt=xt: e.activation(b1(B1_H, D), xt, AF.Identity, bias=lns[:, 8:9], scale=lns[:, 7:8]))
                dep(r=[r1(B1_H, B1_H + D), ("reps", 0, 1)], w=[r1(B1_H, B1_H + D)])
                C_("vector", lambda e: e.tensor_mul(b1(B1_H, D), b1(B1_H, D), reps[:, 0, :]))
                dep(r=[r1(B1_H, B1_H + D), ("reps", 1, 2)], w=[r1(B1_H, B1_H + D)])
                C_("vector", lambda e: e.tensor_add(b1(B1_H, D), b1(B1_H, D), reps[:, 1, :]))
                for kc in range(NKC):
                    dep(r=[r1(B1_H, B1_H + D), "idenb"], w=["ps_t2"])
                    C_("tensor", lambda e, kc=kc: e.transpose(ps_t2[:, kc * 128:(kc + 1) * 128], b1(B1_H + kc * 128, 128), idenb[:]))
                dep(r=["ps_t2"], w=[rht])
                C_("scalar", lambda e, tt=tt: e.activation(hT[:, :, tt * 128:(tt + 1) * 128], ps_t2[:].rearrange("p (kc n) -> p kc n", n=128), AFC))
            DBG("hT", hT[:].rearrange("p kc t -> p (kc t)"), (128, NKC * L), BF16, r=["hT"])

        WQ0 = 10240
        wq_emitters = []

        def _emit_gu_slice(wname, dst, dstr, sl):
            def em():
                wsrc = ycubef[:, 4096:4096 + D]
                dep(w=[RYW(8192, 10240)])
                D_("sync", wsrc, IN[wname][sl * 128:(sl + 1) * 128, :])
                quant_block(wsrc, ycube[:, WQ0:WQ0 + D], D, qa, RYW(8192, 10240), RYW(WQ0, WQ0 + D), "qa")
                for kc in range(NKC):
                    dep(r=[RYW(WQ0 + kc * 128, WQ0 + (kc + 1) * 128), "idenb"], w=["ps_t2"])
                    C_("tensor", lambda e, kc=kc: e.transpose(ps_t2[:, kc * 128:(kc + 1) * 128], ycube[:, WQ0 + kc * 128:WQ0 + (kc + 1) * 128], idenb[:]))
                dep(r=["ps_t2"], w=[RYW(13056, 14080)])
                C_("vector", lambda e: e.tensor_copy(ycube[:, 13056:14080], ps_t2[:]))
                dep(r=[RYW(13056, 14080)], w=[(dstr, sl, sl + 1)])
                D_("sync", dst.rearrange("(kc p) n -> p kc n", p=128)[:, :, sl * 128:(sl + 1) * 128],
                   ycube[:, 13056:14080].rearrange("p (kc n) -> p kc n", n=128))
            return em

        def _emit_wd_slice(sl):
            def em():
                for pc, pw in ((0, 1024), (1024, 1024), (2048, 768)):
                    dep(w=[RYW(8192, 10240)])
                    D_("sync", ycubef[:, 4096:4096 + pw], IN["w_down"][sl * 128:(sl + 1) * 128, pc:pc + pw])
                    quant_block(ycubef[:, 4096:4096 + pw], ycube[:, WQ0 + pc:WQ0 + pc + pw], pw, qa, RYW(8192, 10240), RYW(WQ0 + pc, WQ0 + pc + pw), "qa")
                for fc in range(NFC):
                    dep(r=[RYW(WQ0 + fc * 128, WQ0 + (fc + 1) * 128), "idenb"], w=["ps_t2"])
                    C_("tensor", lambda e, fc=fc: e.transpose(ps_t2[:, (fc % 8) * 128:(fc % 8 + 1) * 128], ycube[:, WQ0 + fc * 128:WQ0 + (fc + 1) * 128], idenb[:]))
                    if fc % 8 == 7 or fc == NFC - 1:
                        n0 = (fc // 8) * 8
                        cnt = fc - n0 + 1
                        dep(r=["ps_t2"], w=[RYW(13056 + n0 * 128, 13056 + (n0 + cnt) * 128)])
                        C_("vector", lambda e, n0=n0, cnt=cnt: e.tensor_copy(
                            ycube[:, 13056 + n0 * 128:13056 + (n0 + cnt) * 128], ps_t2[:, 0:cnt * 128]))
                dep(r=[RYW(13056, 13056 + NFC * 128)], w=[("wdqT", sl, sl + 1)])
                D_("sync", wdqT_d.rearrange("(fc p) n -> p fc n", p=128)[:, :, sl * 128:(sl + 1) * 128],
                   ycube[:, 13056:13056 + NFC * 128].rearrange("p (fc n) -> p fc n", n=128))
            return em

        if phase_limit >= 1:
            for (wname, dst, dstr) in (("w_gate", wgqT_d, "wgqT"), ("w_up", wuqT_d, "wuqT")):
                for sl in range(NFC):
                    wq_emitters.append(_emit_gu_slice(wname, dst, dstr, sl))
            for sl in range(NKC):
                wq_emitters.append(_emit_wd_slice(sl))

        def drain_wq(k):
            for _ in range(k):
                if wq_emitters:
                    wq_emitters.pop(0)()

        def fwd_fft(src_off, dst_off):
            src = b2(src_off, L)
            rsrc = r2(src_off, src_off + L)
            for Q in range(16):
                pst, pstr = (ps_t1, "ps_t1") if (Q // 8) % 2 == 0 else (ps_t2, "ps_t2")
                for q in range(4):
                    n2 = 4 * Q + q
                    inap = src.rearrange("p (n1 n2) -> p n2 n1", n2=64)[:, n2, :]
                    dep(r=[rsrc, "idenb"], w=[pstr])
                    C_("tensor", lambda e, q=q, Q=Q, inap=inap, pst=pst: e.transpose(
                        pst[32 * q:32 * (q + 1), (Q % 8) * 128:(Q % 8 + 1) * 128], inap, idenb[:],
                        tile_position=(0, 32 * q)))
                if Q % 8 == 7:
                    dep(r=[pstr], w=[r1(B1_V1 + (Q - 7) * 128, B1_V1 + (Q + 1) * 128)])
                    C_("vector", lambda e, Q=Q, pst=pst: e.tensor_copy(
                        big1[:, B1_V1 + (Q - 7) * 128:B1_V1 + (Q + 1) * 128], pst[:]))
            for Q in range(16):
                ps, psr = (ps_a, "ps_a") if Q % 2 == 0 else (ps_b, "ps_b")
                dep(r=[r1(B1_V1 + Q * 128, B1_V1 + (Q + 1) * 128), "s1cb"], w=[psr])
                C_("tensor", lambda e, Q=Q, ps=ps: e.matmul(ps[:, 0:264], big1[:, B1_V1 + Q * 128:B1_V1 + (Q + 1) * 128], s1cb[:], start=True, stop=True))
                Aap = b2(B2_SA, 4224).rearrange("p (a r n2) -> p a r n2", a=33, r=2)[:, :, :, 4 * Q:4 * Q + 4].transpose([0, 3, 1, 2])
                dep(r=[psr], w=[r2(B2_SA, B2_SA + 4224)])
                C_("scalar", lambda e, Aap=Aap, ps=ps: e.activation(Aap, ps[:, 0:264].rearrange("p (q a r) -> p q a r", q=4, a=33), AFC))
            for a in range(33):
                pst, pstr = (ps_t1, "ps_t1") if (a // 8) % 2 == 0 else (ps_t2, "ps_t2")
                inap = b2(B2_SA + a * 128, 128)
                dep(r=[r2(B2_SA + a * 128, B2_SA + (a + 1) * 128), "idenb"], w=[pstr])
                C_("tensor", lambda e, a=a, inap=inap, pst=pst: e.transpose(pst[:, (a % 8) * 128:(a % 8 + 1) * 128], inap, idenb[:]))
                if a % 8 == 7 or a == 32:
                    a0 = (a // 8) * 8
                    cnt = a - a0 + 1
                    dep(r=[pstr], w=[r2(B2_SB + a0 * 128, B2_SB + (a0 + cnt) * 128)])
                    C_("vector", lambda e, a0=a0, cnt=cnt, pst=pst: e.tensor_copy(
                        big2[:, B2_SB + a0 * 128:B2_SB + (a0 + cnt) * 128], pst[:, 0:cnt * 128]))
            for a in range(33):
                ps, psr = (ps_a, "ps_a") if (a // 4) % 2 == 0 else (ps_b, "ps_b")
                dep(r=[r2(B2_SB + a * 128, B2_SB + (a + 1) * 128), "uhatb"], w=[psr])
                C_("tensor", lambda e, a=a, ps=ps: e.matmul(ps[:, (a % 4) * 128:(a % 4 + 1) * 128], big2[:, B2_SB + a * 128:B2_SB + (a + 1) * 128], uhatb[:, a * 128:(a + 1) * 128], start=True, stop=True))
                if a % 4 == 3 or a == 32:
                    a0 = (a // 4) * 4
                    cnt = a - a0 + 1
                    dep(r=[psr], w=[r2(dst_off + a0 * 128, dst_off + (a0 + cnt) * 128)])
                    C_("scalar", lambda e, a0=a0, cnt=cnt, ps=ps: e.activation(b2(dst_off + a0 * 128, cnt * 128), ps[:, 0:cnt * 128], AFC))

        B1_W2 = 10252

        def inv_fft(y_off, g, vo, xo):
            for a in range(33):
                pst, pstr = (ps_t1, "ps_t1") if (a // 8) % 2 == 0 else (ps_t2, "ps_t2")
                inap = b2(y_off + a * 128, 128)
                dep(r=[r2(y_off + a * 128, y_off + (a + 1) * 128), "idenb"], w=[pstr])
                C_("tensor", lambda e, a=a, inap=inap, pst=pst: e.transpose(pst[:, (a % 8) * 128:(a % 8 + 1) * 128], inap, idenb[:]))
                if a % 8 == 7 or a == 32:
                    a0 = (a // 8) * 8
                    cnt = a - a0 + 1
                    dep(r=[pstr], w=[r2(B2_SB + a0 * 128, B2_SB + (a0 + cnt) * 128)])
                    C_("vector", lambda e, a0=a0, cnt=cnt, pst=pst: e.tensor_copy(
                        big2[:, B2_SB + a0 * 128:B2_SB + (a0 + cnt) * 128], pst[:, 0:cnt * 128]))
            for a in range(33):
                ps, psr = (ps_a, "ps_a") if (a // 4) % 2 == 0 else (ps_b, "ps_b")
                dep(r=[r2(B2_SB + a * 128, B2_SB + (a + 1) * 128), "w1b"], w=[psr])
                C_("tensor", lambda e, a=a, ps=ps: e.matmul(ps[:, (a % 4) * 128:(a % 4 + 1) * 128], big2[:, B2_SB + a * 128:B2_SB + (a + 1) * 128], w1b[:], start=True, stop=True))
                if a % 4 == 3 or a == 32:
                    a0 = (a // 4) * 4
                    cnt = a - a0 + 1
                    Dap = b2(B2_K2, 4224).rearrange("p (n2o a r) -> p n2o a r", n2o=64, a=33)[:, :, a0:a0 + cnt, :].transpose([0, 2, 1, 3])
                    dep(r=[psr], w=[r2(B2_K2, B2_K2 + 4224)])
                    C_("scalar", lambda e, Dap=Dap, cnt=cnt, ps=ps: e.activation(Dap, ps[:, 0:cnt * 128].rearrange("p (a4 n2o r) -> p a4 n2o r", n2o=64, r=2), AFC))
            for strip in range(8):
                for j in range(8):
                    n2o = strip * 8 + j
                    inap = b2(B2_K2 + n2o * 66, 66)
                    dep(r=[r2(B2_K2 + n2o * 66, B2_K2 + n2o * 66 + 66), "idenb"], w=["ps_t2"])
                    C_("tensor", lambda e, j=j, inap=inap: e.transpose(ps_t2[0:66, j * 128:(j + 1) * 128], inap, idenb[:]))
                dep(r=["ps_t2"], w=[r1(B1_V1, B1_V1 + 1024)])
                C_("vector", lambda e: e.tensor_copy(big1[0:66, B1_V1:B1_V1 + 1024], ps_t2[0:66, :]))
                for j in range(8):
                    n2o = strip * 8 + j
                    dep(r=[r1(B1_V1, B1_V1 + 1024), r1(B1_W2, B1_W2 + 2048)], w=["ps_w"])
                    C_("tensor", lambda e, j=j, n2o=n2o: e.matmul(ps_w[:, j * 32:(j + 1) * 32], big1[0:66, B1_V1 + j * 128:B1_V1 + (j + 1) * 128], big1[0:66, B1_W2 + n2o * 32:B1_W2 + (n2o + 1) * 32], start=True, stop=True))
                def tv(t, s):
                    return t.rearrange("p (m n2) -> p n2 m", n2=64)[:, s * 8:(s + 1) * 8, :]
                dep(r=["ps_w", r2(vo, vo + L), "fbt"], w=[("ycube", g * L, (g + 1) * L)])
                C_("vector", lambda e, s=strip: e.scalar_tensor_tensor(
                    tv(ycube[:, g * L:(g + 1) * L], s), tv(b2(vo, L), s), fbt[:, g:g + 1],
                    ps_w[:, 0:256].rearrange("p (j m) -> p j m", j=8), ALU.mult, ALU.add))
            dep(r=[("ycube", g * L, (g + 1) * L), r2(xo, xo + L)], w=[("ycube", g * L, (g + 1) * L)])
            C_("vector", lambda e: e.tensor_mul(ycube[:, g * L:(g + 1) * L], ycube[:, g * L:(g + 1) * L], b2(xo, L)))

        if phase_limit >= 4:
            inw3 = big3[:, 0:NKC * 384].rearrange("p (kc s n) -> p kc s n", s=3, n=128)

            def inproj(g):
                for s_ in range(3):
                    dep(w=[r3(s_ * 1024, s_ * 1024 + 1024)])
                    D_("gpsimd", inw3[:, :, s_, :],
                       IN["in_w"].rearrange("(kc p) (s nn) -> p kc s nn", p=128, nn=1024)[:, :, s_, g * 128:(g + 1) * 128])
                for s in range(3):
                    cc = s * 8 + g
                    dep(r=["inbt"], w=["lns"])
                    C_("vector", lambda e, cc=cc: e.tensor_scalar_mul(lns[:, 10:11], inbt[:, cc:cc + 1], -1.0))
                    dep(r=["lns"], w=[r1(B1_PAD[s], B1_PAD[s] + 2)])
                    C_("vector", lambda e, s=s: e.tensor_copy(big1[:, B1_PAD[s]:B1_PAD[s] + 2], lns[:, 10:11].broadcast_to([128, 2])))
                    for tc in range(4):
                        pso = ps_x[:, 0:512] if tc % 2 == 0 else ps_x[:, 512:1024]
                        psr = "ps_x"
                        for kc in range(NKC):
                            dep(r=[r3(0, 3072), ("hT", tc * 512, (tc + 1) * 512)], w=[psr])
                            C_("tensor", lambda e, s=s, tc=tc, kc=kc, g=g, pso=pso: e.matmul(
                                pso, inw3[:, kc, s, :],
                                hT[:, kc, tc * 512:(tc + 1) * 512], start=(kc == 0), stop=(kc == 7)))
                        dep(r=[psr], w=[r1(B1_PAD[s] + 2 + tc * 512, B1_PAD[s] + 2 + (tc + 1) * 512)])
                        C_("scalar", lambda e, s=s, tc=tc, pso=pso: e.activation(
                            big1[:, B1_PAD[s] + 2 + tc * 512:B1_PAD[s] + 2 + (tc + 1) * 512], pso, AFC))

            def taps(g):
                xb = 0 if g % 2 == 0 else 18816
                for s in range(3):
                    cc = s * 8 + g
                    dsts = [B2_X0 + xb, B2_X1, B2_VIN + xb][s]
                    rpad = r1(B1_PAD[s], B1_PAD[s] + 2 + L)
                    rdst = r2(dsts, dsts + L)
                    dep(r=[rpad, "swt", "bprime"], w=[rdst])
                    C_("vector", lambda e, s=s, cc=cc, dsts=dsts: e.tensor_scalar(
                        b2(dsts, L), big1[:, B1_PAD[s]:B1_PAD[s] + L],
                        swt[:, cc, 0:1], bprime[:, cc:cc + 1], ALU.mult, ALU.add))
                    dep(r=[rpad, "swt", rdst], w=[rdst])
                    C_("vector", lambda e, s=s, cc=cc, dsts=dsts: e.scalar_tensor_tensor(
                        b2(dsts, L), big1[:, B1_PAD[s] + 1:B1_PAD[s] + 1 + L],
                        swt[:, cc, 1:2], b2(dsts, L), ALU.mult, ALU.add))
                    dep(r=[rpad, "swt", rdst], w=[rdst])
                    C_("vector", lambda e, s=s, cc=cc, dsts=dsts: e.scalar_tensor_tensor(
                        b2(dsts, L), big1[:, B1_PAD[s] + 2:B1_PAD[s] + 2 + L],
                        swt[:, cc, 2:3], b2(dsts, L), ALU.mult, ALU.add))
                dep(r=[r2(B2_VIN + xb, B2_VIN + xb + L), r2(B2_X1, B2_X1 + L)], w=[r2(B2_VIN + xb, B2_VIN + xb + L)])
                C_("vector", lambda e, xb=xb: e.tensor_mul(b2(B2_VIN + xb, L), b2(B2_VIN + xb, L), b2(B2_X1, L)))
                if g == 0:
                    DBG("vin0", b2(B2_VIN, L), (128, L), BF16, r=[r2(B2_VIN, B2_VIN + L)])
                    DBG("x00", b2(B2_X0, L), (128, L), BF16, r=[r2(B2_X0, B2_X0 + L)])

            def kdense(g):
                xb = 0
                dep(r=[R_THAT, "acol"], w=[r1(B1_DEC, B1_DEC + L)])
                C_("scalar", lambda e, g=g: e.activation(b1(B1_DEC, L), thatrep, AF.Exp, scale=acol[:, g:g + 1]))
                for qh in range(4):
                    pso, psr = (ps_a, "ps_a") if qh % 2 == 0 else (ps_b, "ps_b")
                    dep(r=["w3b", "h3"], w=[psr])
                    C_("tensor", lambda e, g=g, qh=qh, pso=pso: e.matmul(
                        pso[:, 0:512], w3b[0:64, g * 128:(g + 1) * 128], h3[0:64, qh * 512:(qh + 1) * 512], start=True, stop=True))
                    dep(r=[psr, r1(B1_DEC + qh * 512, B1_DEC + (qh + 1) * 512)], w=[r2(B2_X1 + qh * 512, B2_X1 + (qh + 1) * 512)])
                    C_("vector", lambda e, qh=qh, pso=pso: e.tensor_mul(big2[:, B2_X1 + qh * 512:B2_X1 + (qh + 1) * 512], pso[:, 0:512], big1[:, B1_DEC + qh * 512:B1_DEC + (qh + 1) * 512]))
                if g == 0:
                    DBG("kdense0", b2(B2_X1, L), (128, L), BF16, r=[r2(B2_X1, B2_X1 + L)])

            inproj(0)
            taps(0)
            kdense(0)
            for g in range(NG):
                xb = 0 if g % 2 == 0 else 18816
                fwd_fft(B2_X1, B2_K2)
                fwd_fft(B2_VIN + xb, B2_U)
                if g == 0:
                    DBG("U0", b2(B2_U, 4224), (128, 4224), BF16, r=[r2(B2_U, B2_U + 4224)])
                    DBG("K0", b2(B2_K2, 4224), (128, 4224), BF16, r=[r2(B2_K2, B2_K2 + 4224)])
                def vw(off, r):
                    return b2(off, 4224).rearrange("p (a r n) -> p a r n", a=33, r=2)[:, :, r, :]
                RU = r2(B2_U, B2_U + 4224)
                RK = r2(B2_K2, B2_K2 + 4224)
                sp1 = b2(B2_SB, 2112).rearrange("p (a n) -> p a n", a=33)
                sp2 = b2(B2_SB + 2112, 2112).rearrange("p (a n) -> p a n", a=33)
                dep(r=[RU, RK], w=[r2(B2_SB, B2_SB + 2112)])
                C_("vector", lambda e: e.tensor_tensor(sp1, vw(B2_U, 0), vw(B2_K2, 1), ALU.mult))
                dep(r=[RU, RK], w=[r2(B2_SB + 2112, B2_SB + 4224)])
                C_("gpsimd", lambda e: e.tensor_tensor(sp2, vw(B2_U, 1), vw(B2_K2, 1), ALU.mult))
                dep(r=[RU, RK], w=[RU])
                C_("vector", lambda e: e.tensor_tensor(vw(B2_U, 1), vw(B2_U, 1), vw(B2_K2, 0), ALU.mult))
                dep(r=[RU, RK], w=[RU])
                C_("gpsimd", lambda e: e.tensor_tensor(vw(B2_U, 0), vw(B2_U, 0), vw(B2_K2, 0), ALU.mult))
                dep(r=[RU, r2(B2_SB + 2112, B2_SB + 4224)], w=[RU])
                C_("gpsimd", lambda e: e.tensor_tensor(vw(B2_U, 0), vw(B2_U, 0), sp2, ALU.subtract))
                dep(r=[RU, r2(B2_SB, B2_SB + 2112)], w=[RU])
                C_("vector", lambda e: e.tensor_tensor(vw(B2_U, 1), vw(B2_U, 1), sp1, ALU.add))
                if g == 0:
                    DBG("Ymul", b2(B2_U, 4224), (128, 4224), BF16, r=[RU])
                if g + 1 < NG:
                    inproj(g + 1)
                dep(r=["w2b16d"], w=[r1(B1_W2, B1_W2 + 2048)])
                D_("sync", big1[:, B1_W2:B1_W2 + 2048], w2b16_d)
                if g == 0:
                    DBG("w2g", big1[:, B1_W2:B1_W2 + 2048], (128, 2048), BF16, r=[r1(B1_W2, B1_W2 + 2048)])
                if g + 1 < NG:
                    taps(g + 1)
                    kdense(g + 1)
                inv_fft(B2_U, g, B2_VIN + xb, B2_X0 + xb)
                if g == 0:
                    DBG("y0", ycube[:, 0:L], (128, L), BF16, r=[("ycube", 0, L)])
                drain_wq(13)

        if phase_limit >= 5:
            load_reps([(2, 0)])
            for hf in range(2):
                dep(w=[r3(0, 4096)])
                D_("gpsimd", big3[:, 0:4096].rearrange("p (kc n) -> p kc n", n=1024),
                   IN["out_w"].rearrange("(kc p) n -> p kc n", p=128)[:, hf * 4:(hf + 1) * 4, :])
                for kc4 in range(4):
                    kc = hf * 4 + kc4
                    dep(r=[r3(kc4 * 1024, (kc4 + 1) * 1024), ("reps", 0, 1)], w=[r1(kc * 1024, (kc + 1) * 1024)])
                    C_("vector", lambda e, kc=kc, kc4=kc4: e.tensor_mul(big1[:, kc * 1024:(kc + 1) * 1024], big3[:, kc4 * 1024:(kc4 + 1) * 1024], reps[:, 0, :]))
            dep(w=["qa"])
            D_("sync", qa[0:1, 0:1024], IN["out_b"])
            dep(r=["qa"], w=["qcb"])
            C_("vector", lambda e: e.tensor_copy(qcb[0:1, 0:1024], qa[0:1, 0:1024]))
            for hh in range(2):
                dep(r=["qcb", "onesrow"], w=["ps_w"])
                C_("tensor", lambda e, hh=hh: e.matmul(ps_w[:, hh * 512:(hh + 1) * 512], onesrow[:], qcb[0:1, hh * 512:(hh + 1) * 512], start=True, stop=True))
            dep(r=["ps_w", ("reps", 0, 1)], w=[r1(8192, 9216)])
            C_("vector", lambda e: e.tensor_mul(b1(8192, 1024), ps_w[:].rearrange("p n -> p n"), reps[:, 0, :]))
            for tt in range(NT):
                psp, pspr = (ps_w, "ps_w") if tt % 2 == 0 else (ps_x, "ps_x")
                for kc in range(NKC):
                    for hh in range(2):
                        dep(r=["ycube", r1(0, 8192)], w=[pspr])
                        C_("tensor", lambda e, tt=tt, kc=kc, hh=hh, psp=psp: e.matmul(
                            psp[:, hh * 512:(hh + 1) * 512], ycube[:, kc * L + tt * 128:kc * L + (tt + 1) * 128],
                            big1[:, kc * 1024 + hh * 512:kc * 1024 + (hh + 1) * 512],
                            start=(kc == 0), stop=(kc == 7)))
                dep(r=[pspr, ("xres", tt, tt + 1)], w=[("xres", tt, tt + 1)])
                C_("vector", lambda e, tt=tt, psp=psp: e.tensor_add(xres[:, tt, :], xres[:, tt, :], psp[:]))
                dep(r=[r1(8192, 9216), ("xres", tt, tt + 1)], w=[("xres", tt, tt + 1)])
                C_("vector", lambda e, tt=tt: e.tensor_add(xres[:, tt, :], xres[:, tt, :], b1(8192, 1024)))
            DBG("xattn", xres[:].rearrange("p T d -> p (T d)"), (128, NT * D), BF16, r=["xres"])

        drain_wq(99)
        if phase_limit >= 6:
            load_reps([(4, 0), (3, 1)])
            for tt in range(NT):
                xt = xres[:, tt, :]
                rx = ("xres", tt, tt + 1)
                rht = ("hT", tt * 128, (tt + 1) * 128)
                ln_stats(xt, tt)
                dep(r=[rx, "lns"], w=["qa"])
                C_("scalar", lambda e, xt=xt: e.activation(qa[:, 0:D], xt, AF.Identity, bias=lns[:, 8:9], scale=lns[:, 7:8]))
                dep(r=["qa", ("reps", 0, 1)], w=["qa"])
                C_("vector", lambda e: e.tensor_mul(qa[:, 0:D], qa[:, 0:D], reps[:, 0, :]))
                dep(r=["qa", ("reps", 1, 2)], w=["qa"])
                C_("vector", lambda e: e.tensor_add(qa[:, 0:D], qa[:, 0:D], reps[:, 1, :]))
                quant_block(qa[:, 0:D], qcb[:, 0:D], D, big1f[:, 0:1024], "qa", "qcb", r1(0, 2048))
                for kc in range(NKC):
                    dep(r=["qcb", "idenb"], w=["ps_t2"])
                    C_("tensor", lambda e, kc=kc: e.transpose(ps_t2[:, kc * 128:(kc + 1) * 128], qcb[:, kc * 128:(kc + 1) * 128], idenb[:]))
                dep(r=["ps_t2"], w=[rht])
                C_("scalar", lambda e, tt=tt: e.activation(hT[:, :, tt * 128:(tt + 1) * 128], ps_t2[:].rearrange("p (kc n) -> p kc n", n=128), AFC))
            DBG("h2qT", hT[:].rearrange("p kc t -> p (kc t)"), (128, NKC * L), BF16, r=["hT"])

        if phase_limit >= 7:
            def hqT_ap(tth, fc):
                idx = tth * NFC + fc
                if idx < 128:
                    return ycube[:, idx * 128:(idx + 1) * 128], ("ycube", idx * 128, (idx + 1) * 128)
                idx -= 128
                return big1[:, idx * 128:(idx + 1) * 128], ("big1", idx * 128, (idx + 1) * 128)

            load_reps([(5, 0)])
            dep(r=["wdqT"], w=["big2"])
            D_("sync", big2[:, 0:NFC * 1024].rearrange("p (fc n) -> p fc n", n=1024),
               wdqT_d.rearrange("(fc p) n -> p fc n", p=128))
            wgs = big3[:, 0:4096]
            wus = big1[:, 8192:12288]
            CHUNKS = [(i * 512, 512) for i in range(5)] + [(2560, 256)]
            chain_i = 0
            for half in range(2):
                for (n0, cw) in CHUNKS:
                    dep(r=["wgqT"], w=[r3(0, 4096)])
                    D_("sync", wgs[:, 0:NKC * cw].rearrange("p (kc n) -> p kc n", n=cw),
                       wgqT_d.rearrange("(kc p) n -> p kc n", p=128)[:, :, n0:n0 + cw])
                    dep(r=["wuqT"], w=[r1(8192, 12288)])
                    D_("gpsimd", wus[:, 0:NKC * cw].rearrange("p (kc n) -> p kc n", n=cw),
                       wuqT_d.rearrange("(kc p) n -> p kc n", p=128)[:, :, n0:n0 + cw])
                    for tth in range(8):
                        tt = half * 8 + tth
                        hq = chain_i % 2
                        chain_i += 1
                        ho = hq * 512
                        psg, psgr = (ps_a, "ps_a") if hq == 0 else (ps_b, "ps_b")
                        psu, psur = (ps_w, "ps_w") if hq == 0 else (ps_x, "ps_x")
                        pst, pstr = (ps_t2, "ps_t2") if hq == 0 else (ps_t1, "ps_t1")
                        rht = ("hT", tt * 128, (tt + 1) * 128)
                        for kc in range(NKC):
                            dep(r=[rht, r3(0, 4096)], w=[psgr])
                            C_("tensor", lambda e, tt=tt, kc=kc, cw=cw, psg=psg: e.matmul(
                                psg[:, 0:cw], hT[:, kc, tt * 128:(tt + 1) * 128],
                                wgs[:, kc * cw:(kc + 1) * cw], start=(kc == 0), stop=(kc == 7)))
                        for kc in range(NKC):
                            dep(r=[rht, r1(8192, 12288)], w=[psur])
                            C_("tensor", lambda e, tt=tt, kc=kc, cw=cw, psu=psu: e.matmul(
                                psu[:, 0:cw], hT[:, kc, tt * 128:(tt + 1) * 128],
                                wus[:, kc * cw:(kc + 1) * cw], start=(kc == 0), stop=(kc == 7)))
                        dep(r=[psgr], w=[("qcc", ho, ho + cw)])
                        C_("scalar", lambda e, psg=psg, ho=ho, cw=cw: e.activation(qcc[:, ho:ho + cw], psg[:, 0:cw], AF.Silu))
                        dep(r=[("qcc", ho, ho + cw), psur], w=[("qa", ho, ho + cw)])
                        C_("vector", lambda e, psu=psu, ho=ho, cw=cw: e.tensor_tensor(qa[:, ho:ho + cw], qcc[:, ho:ho + cw], psu[:, 0:cw], ALU.mult))
                        if half == 0 and n0 == 0 and tth == 0:
                            DBG("hmid0", qa[:, 0:256], (128, 256), r=["qa"])
                        quant_block(qa[:, ho:ho + cw], qcb[:, ho:ho + cw], cw, big1f[:, 3072 + hq * 512:3584 + hq * 512], ("qa", ho, ho + cw), ("qcb", ho, ho + cw), r1(6144 + hq * 1024, 7168 + hq * 1024), half=hq)
                        for j in range(cw // 128):
                            dep(r=[("qcb", ho, ho + cw), "idenb"], w=[pstr])
                            C_("tensor", lambda e, j=j, ho=ho, pst=pst: e.transpose(pst[:, j * 128:(j + 1) * 128], qcb[:, ho + j * 128:ho + (j + 1) * 128], idenb[:]))
                        for j in range(cw // 128):
                            fc = n0 // 128 + j
                            dst, dstr = hqT_ap(tth, fc)
                            dep(r=[pstr], w=[dstr])
                            C_("vector", lambda e, j=j, dst=dst, pst=pst: e.tensor_copy(dst, pst[:, j * 128:(j + 1) * 128]))
                for tth in range(8):
                    tt = half * 8 + tth
                    psp, pspr = (ps_w, "ps_w") if tth % 2 == 0 else (ps_x, "ps_x")
                    for fc in range(NFC):
                        lh, lhr = hqT_ap(tth, fc)
                        for hh in range(2):
                            dep(r=[lhr, ("big2", fc * 1024, (fc + 1) * 1024)], w=[pspr])
                            C_("tensor", lambda e, fc=fc, hh=hh, lh=lh, psp=psp: e.matmul(
                                psp[:, hh * 512:(hh + 1) * 512], lh,
                                big2[:, fc * 1024 + hh * 512:fc * 1024 + (hh + 1) * 512],
                                start=(fc == 0), stop=(fc == NFC - 1)))
                    dep(r=[pspr, ("reps", 0, 1)], w=["qa"])
                    C_("vector", lambda e, psp=psp: e.tensor_mul(qa[:, 0:D], psp[:], reps[:, 0, :]))
                    dep(r=["qa", ("xres", tt, tt + 1)], w=["qa"])
                    C_("vector", lambda e, tt=tt: e.tensor_add(qa[:, 0:D], qa[:, 0:D], xres[:, tt, :]))
                    dep(r=["qa"], w=[("outd", tt, tt + 1)])
                    D_("sync", out_d.rearrange("(T p) d -> p T d", p=128)[:, tt, :], qa[:, 0:D])

        OWN = dict(ENG_SEM)

        def emit(e, name):
            clock = dict.fromkeys(SEMS, 0)
            own = OWN.get(name)
            for (eng, fn, sem, val, deps, inc) in PLAN:
                if eng != name:
                    continue
                for s, v in deps.items():
                    if s == own and own == "ct":
                        continue
                    if v > clock[s]:
                        e.wait_ge(SEMOBJ[s], v)
                        clock[s] = v
                fn(e).then_inc(SEMOBJ[sem], inc)
                if sem == own and own == "ct":
                    clock[sem] = val

        @block.sync
        def _(e):
            emit(e, "sync")
            for s in SEMS:
                if cum[s]:
                    e.wait_ge(SEMOBJ[s], cum[s])

        @block.vector
        def _(e):
            emit(e, "vector")

        @block.scalar
        def _(e):
            emit(e, "scalar")

        @block.gpsimd
        def _(e):
            emit(e, "gpsimd")

        @block.tensor
        def _(e):
            emit(e, "tensor")

    _ctx.close()
    return nc


def _host_inputs(inputs, b):
    cc = _build_consts()
    sw = np.asarray(inputs["L0_attn_short_w"], np.float32).reshape(24, 128, 3).transpose(1, 0, 2).reshape(128, 72)
    inb = np.asarray(inputs["L0_attn_in_b"], np.float32).reshape(24, 128).T
    sb = np.asarray(inputs["L0_attn_short_b"], np.float32).reshape(24, 128).T
    fb = np.asarray(inputs["L0_attn_filter_bias"], np.float32).reshape(8, 128).T
    m = {
        "u": np.ascontiguousarray(np.asarray(inputs["L0_attn_u"], np.float32)[b]),
        "c_row": np.ascontiguousarray(np.asarray(inputs["L0_c"], np.float32)[b:b + 1]),
        "in_w": np.asarray(inputs["L0_attn_in_w"], np.float32),
        "swp": np.ascontiguousarray(sw), "inbp": np.ascontiguousarray(inb),
        "sbp": np.ascontiguousarray(sb), "fbp": np.ascontiguousarray(fb),
        "mlp_w0": np.asarray(inputs["L0_attn_mlp_w0"], np.float32),
        "mlp_b0": np.ascontiguousarray(np.asarray(inputs["L0_attn_mlp_b0"], np.float32).reshape(-1)[:, None]),
        "freq0": np.ascontiguousarray(np.asarray(inputs["L0_attn_freq0"], np.float32).reshape(-1)[:, None]),
        "mlp_w1": np.asarray(inputs["L0_attn_mlp_w1"], np.float32),
        "mlp_b1": np.ascontiguousarray(np.asarray(inputs["L0_attn_mlp_b1"], np.float32).reshape(-1)[:, None]),
        "freq1": np.ascontiguousarray(np.asarray(inputs["L0_attn_freq1"], np.float32).reshape(-1)[:, None]),
        "mlp_w2": np.asarray(inputs["L0_attn_mlp_w2"], np.float32),
        "mlp_b2": np.ascontiguousarray(np.asarray(inputs["L0_attn_mlp_b2"], np.float32).reshape(-1)[:, None]),
        "freq2": np.ascontiguousarray(np.asarray(inputs["L0_attn_freq2"], np.float32).reshape(-1)[:, None]),
        "mlp_w3": np.asarray(inputs["L0_attn_mlp_w3"], np.float32),
        "out_w": np.asarray(inputs["L0_attn_out_w"], np.float32),
        "out_b": np.asarray(inputs["L0_attn_out_b"], np.float32)[None, :],
        "w_gate": np.asarray(inputs["L0_ffn_w_gate"], np.float32),
        "w_up": np.asarray(inputs["L0_ffn_w_up"], np.float32),
        "w_down": np.asarray(inputs["L0_ffn_w_down"], np.float32),
        "wmod": np.asarray(inputs["L0_wmod"], np.float32),
    }
    m.update(cc)
    return m


def kernel(**inputs):
    nc = build_program()
    in_maps = [_host_inputs(inputs, b) for b in range(8)]
    res = run_bass_kernel_spmd(nc, in_maps, core_ids=list(range(8)))
    out = np.stack([np.asarray(res.results[b]["out"], np.float32) for b in range(8)], axis=0)
    return out
